# version 09
# baseline (speedup 1.0000x reference)
import numpy as np
import ml_dtypes
import concourse.bass as bass
import concourse.mybir as mybir
from concourse.bass_utils import run_bass_kernel_spmd

F32 = mybir.dt.float32
BF16 = mybir.dt.bfloat16
AF = mybir.ActivationFunctionType
ALU = mybir.AluOpType
AX = mybir.AxisListType

D, DI, DS, DTR, DFF = 1024, 2048, 16, 64, 4096
PD, NDB, MF = D // 128, DI // 128, DFF // 128
ENGS = ("sync", "tensor", "vector", "scalar", "gpsimd")

BIGW = {
    "w1": (DFF, D), "w2": (DFF, D), "h1": (DFF, DFF), "h2": (DFF, DFF),
    "w3": (D, DFF), "ff1": (DFF, D), "ff2": (D, DFF),
    "in1": (2 * DI, D), "in2": (2 * DI, D), "out1": (D, DI), "out2": (D, DI),
}
SMALLW = [("alog1", DS * NDB), ("alog2", DS * NDB), ("cw1", 4 * NDB),
          ("cw2", 4 * NDB), ("cb1", NDB), ("cb2", NDB), ("dtb1", NDB),
          ("dtb2", NDB), ("dp1", NDB), ("dp2", NDB), ("s_w1", MF),
          ("s_w2", MF), ("s_h1", MF), ("s_h2", MF), ("s_w3", PD),
          ("ng", PD), ("nb", PD), ("fg", PD), ("fb", PD),
          ("ff1b", MF), ("ff2b", PD)]
SCW = dict(SMALLW)

_DEV = 0
_PHASES = 99


class Plan:
    def __init__(self, nc):
        self.nc = nc
        self.q = {e: [] for e in ENGS}
        self.cnt = {}
        self.sems = {}

    def sem(self, name):
        self.sems[name] = self.nc.semaphore(name).__enter__()
        self.cnt[name] = 0

    def emit(self, eng, fn):
        self.q[eng].append(fn)

    def wait(self, eng, name, val=None):
        v = self.cnt[name] if val is None else val
        if v <= 0:
            return
        s = self.sems[name]
        self.emit(eng, lambda E, s=s, v=v: E.wait_ge(s, v))

    def dma(self, eng, out, in_, sem="sDM", **kw):
        s = self.sems[sem]
        self.cnt[sem] += 16
        self.emit(eng, lambda E, o=out, i=in_, s=s, kw=kw:
                  E.dma_start(out=o, in_=i, **kw).then_inc(s, 16))
        return self.cnt[sem]

    ENGSEM = {"scalar": "sSC", "vector": "sEV", "gpsimd": "sGP"}

    def op(self, eng, sem, fn):
        auto = self.ENGSEM.get(eng)
        if auto is not None:
            assert sem is None or sem == auto, (eng, sem)
            prev = self.cnt[auto]
            if prev > 0:
                sa = self.sems[auto]
                self.emit(eng, lambda E, s=sa, v=prev: E.wait_ge(s, v))
            sem = auto
        s = self.sems[sem] if sem else None
        if sem:
            self.cnt[sem] += 1

        def run(E, fn=fn, s=s):
            inst = fn(E)
            if s is not None:
                inst.then_inc(s, 1)
        self.emit(eng, run)
        return self.cnt[sem] if sem else None

    GPD = ("sLD", "sSP", "sHW", "sZR0", "sZR1", "sZS0", "sZS1",
           "sOS0", "sOS1", "sST0", "sST1", "sDM", "sOUT")
    SYD = ("sWsl0", "sWsl1", "sWsl2", "sWsl3")

    def barrier(self):
        s = self.sems["ph"]
        target = self.cnt["ph"] + len(ENGS)
        self.cnt["ph"] = target
        for e in ENGS:
            auto = self.ENGSEM.get(e)
            if e == "tensor":
                auto = "sMM"
            if auto is not None and self.cnt[auto] > 0:
                sa, va = self.sems[auto], self.cnt[auto]
                self.emit(e, lambda E, s=sa, v=va: E.wait_ge(s, v))
            drains = self.GPD if e == "gpsimd" else (self.SYD if e == "sync" else ())
            for dn in drains:
                if self.cnt.get(dn, 0) > 0:
                    self.wait(e, dn, self.cnt[dn])
            self.emit(e, lambda E, s=s: E.nop().then_inc(s, 1))
            self.emit(e, lambda E, s=s, t=target: E.wait_ge(s, t))


def build(T=4096):
    CH = T // 16
    NCH = 16
    EXT0 = 8
    Q = T // 8
    OWN = T // 2
    NC = min(512, Q)
    NB = Q // NC
    assert NB == 1 and CH <= 512

    nc = bass.Bass("TRN2", target_bir_lowering=False, debug=False)
    p = Plan(nc)
    for s in ["ph", "sMM", "sEV", "sSC", "sGP", "sDM", "sLD", "sSP", "sOUT",
              "sWsl0", "sWsl1", "sWsl2", "sWsl3", "sHW",
              "sZR0", "sZR1", "sZS0", "sZS1", "sOS0", "sOS1", "sST0", "sST1",
              "sDM2"]:
        p.sem(s)

    h = {}
    h["xt"] = nc.dram_tensor("xt", [D, T], F32, kind="ExternalInput").ap()
    h["mask"] = nc.dram_tensor("mask", [1, T], F32, kind="ExternalInput").ap()
    h["esel"] = nc.dram_tensor("esel", [DS, DS * 128], F32, kind="ExternalInput").ap()
    for w, (o, i) in BIGW.items():
        h[w] = nc.dram_tensor(f"w_{w}", [(o // 128) * (i // 128) * 128 * 128], F32,
                              kind="ExternalInput").ap()
    h["xp1"] = nc.dram_tensor("xp1", [128, 16 * 96], F32, kind="ExternalInput").ap()
    h["xp2"] = nc.dram_tensor("xp2", [128, 16 * 96], F32, kind="ExternalInput").ap()
    h["dtw1"] = nc.dram_tensor("dtw1", [DTR, DI], F32, kind="ExternalInput").ap()
    h["dtw2"] = nc.dram_tensor("dtw2", [DTR, DI], F32, kind="ExternalInput").ap()
    for nm, wdt in SMALLW:
        h[nm] = nc.dram_tensor(nm, [128, wdt], F32, kind="ExternalInput").ap()

    out_h = nc.dram_tensor("out", [D, OWN], F32, kind="ExternalOutput").ap()
    stash = {}
    for w, (o, i) in BIGW.items():
        stash[w] = nc.dram_tensor(f"st_{w}", [(o // 128) * (i // 128) * 128 * 128],
                                  BF16, kind="Internal").ap()
    kd1 = "ExternalOutput" if _DEV in (1, 3) else "Internal"
    kd2 = "ExternalOutput" if _DEV in (2, 3) else "Internal"
    hT_sp = nc.dram_tensor("hT_sp", [D, T], BF16, kind=kd1).ap()
    x1_sp = nc.dram_tensor("x1_sp", [D, T], F32, kind=kd1).ap()
    xo_sp = nc.dram_tensor("xo_sp", [D, T], F32, kind=kd1).ap()
    z_sp = nc.dram_tensor("z_sp", [2 * DI, T], F32, kind="Internal").ap()
    mout = [nc.dram_tensor("moutA", [D, OWN], F32, kind=kd2).ap(),
            nc.dram_tensor("moutB", [D, OWN], F32, kind=kd2).ap()]
    x2_sp = nc.dram_tensor("x2_sp", [D, OWN], F32, kind="Internal").ap()
    if _DEV >= 2:
        dbg = {nm: nc.dram_tensor("dbg_" + nm, [DI, CH], F32, kind="ExternalOutput").ap()
               for nm in ("xs", "dt", "y", "u")}
        dbg["bc"] = nc.dram_tensor("dbg_bc", [2 * DS, CH], F32, kind="ExternalOutput").ap()
    else:
        dbg = None

    P = {}

    def alloc(name, shape, dtype):
        P[name] = nc.sbuf_tensor(name, shape, dtype).__enter__().ap()
        return P[name]

    SC = {}
    off = 0
    for nm in ["s_w1", "s_w2", "s_h1", "s_h2", "s_w3", "hs3", "ng", "nb",
               "fg", "fb", "ff1b", "ff2b"]:
        SC[nm] = off
        off += SCW.get(nm, PD)
    alloc("scol", [128, off], F32)
    alloc("onesr", [1, 128], F32)
    alloc("onesc", [128, 1], F32)
    alloc("negcol", [128, 1], F32)
    alloc("cwcol", [128, 2 * NDB * 4], F32)
    alloc("cbcol", [128, 2 * NDB], F32)
    alloc("dtbcol", [128, 2 * NDB], F32)
    alloc("dpcol", [128, 2 * NDB], F32)
    alloc("acol", [128, 2 * NDB * DS], F32)
    alloc("state", [128, NDB * DS], F32)
    alloc("halo", [128, NDB * 4], F32)
    alloc("epsc", [1, 1], F32)

    AW = 46080
    arena = nc.sbuf_tensor("arena", [128, AW], F32).__enter__().ap()
    arena_b = arena.bitcast(BF16)

    class Reg(dict):
        def __init__(self, base=0):
            super().__init__()
            self.off = base

        def add(self, name, words, dtype=F32, pdim=128):
            o = self.off
            self.off += words
            assert self.off <= AW, (name, self.off)
            self.at(name, o, words, dtype, pdim)
            return self[name]

        def at(self, name, o, words, dtype=F32, pdim=128):
            if dtype == F32:
                self[name] = arena[0:pdim, o:o + words]
            else:
                self[name] = arena_b[0:pdim, 2 * o:2 * (o + words)]
            return self[name]

    C0 = Reg(0)
    C0.add("stat", 2 * NB * NC)
    SQ0 = C0.off
    C0.add("sq", 3 * NC)
    C0.add("t0", 2 * NC)
    C0.add("t1", 2 * NC)
    CEND = C0.off

    def row(i):
        return arena[0:1, SQ0 + i * NC:SQ0 + (i + 1) * NC]

    R1 = Reg(CEND)
    HB0 = R1.off
    R1.add("Hb", MF * Q // 2, BF16)
    GB0 = R1.off
    R1.add("Gb", MF * Q // 2, BF16)
    R1.add("xq", PD * Q)
    R1.add("sl0", MF * 128 // 2, BF16)
    R1.add("sl1", MF * 128 // 2, BF16)
    R1.add("sl2", MF * 128 // 2, BF16)
    R1.add("sl3", MF * 128 // 2, BF16)
    R1.add("stg", 2 * NC)
    R1.add("stgb", PD * NC // 2, BF16)
    R1.at("Hf", HB0, MF * Q // 2)
    R1.at("Gf", GB0, MF * Q // 2)

    SGRP = 4
    NG = DS // SGRP
    R2 = Reg(0)
    R2.add("maskbc", T // 2, BF16)
    R2.add("hwin", PD * CH // 2, BF16)
    R2.add("hrev", PD * CH // 2, BF16)
    R2.add("xspre", NDB * (CH + 4))
    R2.add("xsconv", NDB * CH)
    R2.add("dtb", NDB * CH)
    R2.add("yb", NDB * CH)
    R2.add("ubuf", NDB * CH)
    R2.add("ygb", NDB * CH // 2, BF16)
    DAP0 = R2.off
    R2.add("daP", 2 * SGRP * CH)
    DBXP0 = R2.off
    R2.add("dbxP", 2 * SGRP * CH)
    R2.add("hP", 2 * SGRP * CH)
    R2.add("bbcP", SGRP * CH)
    R2.add("cbcP", SGRP * CH)
    R2.at("xsbf", DAP0, NDB * CH // 2, BF16)
    R2.at("pP", DBXP0, 2 * SGRP * CH)
    R2.at("zrd", DAP0, 2 * CH)
    R2.at("szb", DAP0 + 2 * CH, CH)
    R2.add("sl0", 16 * 128 // 2, BF16)
    R2.add("sl1", 16 * 128 // 2, BF16)
    R2.add("ost", 2 * CH)
    R2.add("zst", 2 * CH)
    R2.add("dbl", CH)
    R2.add("crow", CH, F32, DS)
    R2.add("brow", CH, F32, DS)
    R2.add("dblbf", CH // 2, BF16, DTR)
    R2.add("eselS", DS * 128, F32, DS)
    R2.add("xpT", 16 * 96, BF16)
    R2.add("dtwT", DI, BF16, DTR)
    assert R2.off <= AW, R2.off

    bank_guard = {}
    buf_guard = {}

    def bguard(eng, key):
        if key in bank_guard:
            s, c = bank_guard[key]
            p.wait(eng, s, c)

    def uguard(eng, key):
        if key in buf_guard:
            s, c = buf_guard[key]
            p.wait(eng, s, c)

    ps = [nc.psum_tensor(f"ps{i}", [128, 512], F32).__enter__().ap() for i in range(8)]

    def phase0():
        g = "gpsimd"
        for nm in SC:
            if nm == "hs3":
                continue
            p.dma(g, P["scol"][:, SC[nm]:SC[nm] + SCW[nm]], h[nm][:, 0:SCW[nm]], "sLD")
        for di, sfx in ((0, "1"), (1, "2")):
            p.dma(g, P["cwcol"][:, di * NDB * 4:(di + 1) * NDB * 4], h["cw" + sfx][:], "sLD")
            p.dma(g, P["cbcol"][:, di * NDB:(di + 1) * NDB], h["cb" + sfx][:], "sLD")
            p.dma(g, P["dtbcol"][:, di * NDB:(di + 1) * NDB], h["dtb" + sfx][:], "sLD")
            p.dma(g, P["dpcol"][:, di * NDB:(di + 1) * NDB], h["dp" + sfx][:], "sLD")
            p.dma(g, P["acol"][:, di * NDB * DS:(di + 1) * NDB * DS], h["alog" + sfx][:], "sLD")
        ld = p.cnt["sLD"]
        p.wait("vector", "sLD", ld)
        p.op("vector", "sEV", lambda E: E.memset(P["onesr"][:], 1.0))
        p.op("vector", "sEV", lambda E: E.memset(P["onesc"][:], 1.0))
        p.op("vector", "sEV", lambda E: E.memset(P["negcol"][:], -1.0))
        p.op("vector", "sEV", lambda E: E.memset(P["state"][:], 0.0))
        p.op("vector", "sEV", lambda E: E.memset(P["halo"][:], 0.0))
        p.op("vector", "sEV", lambda E: E.memset(P["epsc"][:], 1e-5))
        p.op("vector", "sEV", lambda E: E.tensor_scalar_mul(
            P["scol"][:, SC["hs3"]:SC["hs3"] + PD],
            P["scol"][:, SC["s_w3"]:SC["s_w3"] + PD], 0.5))
        ev = p.cnt["sEV"]
        p.wait("scalar", "sLD", ld)
        p.wait("scalar", "sEV", ev)
        p.op("scalar", "sSC", lambda E: E.activation(P["acol"][:], P["acol"][:], AF.Exp))
        p.op("scalar", "sSC", lambda E: E.mul(P["acol"][:], P["acol"][:], P["negcol"][:, 0:1]))
        CHK = 1 << 19
        ffnw = ("w1", "w2", "h1", "h2", "w3")
        for w in ffnw + tuple(k for k in BIGW if k not in ffnw):
            o, i = BIGW[w]
            n = (o // 128) * (i // 128) * 128 * 128
            sm = "sDM" if w in ffnw else "sDM2"
            for ofs in range(0, n, CHK):
                ln = min(CHK, n - ofs)
                p.dma(g, stash[w][ofs:ofs + ln], h[w][ofs:ofs + ln], sm)
        p.wait("sync", "sDM", p.cnt["sDM"])
        p.barrier()

    def ln_stats(getsrc):
        p.wait("scalar", "sEV", p.cnt["sEV"])
        p.wait("tensor", "sEV", p.cnt["sEV"])
        for n in range(NB):
            bguard("tensor", 2)
            bguard("tensor", 3)
            for pt in range(PD):
                src = getsrc(pt)[:, n * NC:(n + 1) * NC]
                sqb = C0["sq"][:, (pt % 2) * NC:(pt % 2) * NC + NC]
                uguard("scalar", ("sq", pt % 2))
                c = p.op("scalar", "sSC", lambda E, s=src, d=sqb: E.activation(d, s, AF.Square))
                p.wait("tensor", "sSC", c)
                p.op("tensor", None, lambda E, s=src, f=(pt == 0), l=(pt == PD - 1):
                     E.matmul(ps[2][0:1, 0:NC], P["onesc"][:], s, start=f, stop=l))
                m2 = p.op("tensor", "sMM", lambda E, d=sqb, f=(pt == 0), l=(pt == PD - 1):
                          E.matmul(ps[3][0:1, 0:NC], P["onesc"][:], d, start=f, stop=l))
                buf_guard[("sq", pt % 2)] = ("sMM", m2)
            mmv = p.cnt["sMM"]
            p.wait("vector", "sMM", mmv)
            mu, va, tr = row(0), row(1), row(2)
            p.op("vector", None, lambda E: E.tensor_scalar_mul(mu, ps[2][0:1, 0:NC], 1.0 / D))
            p.op("vector", None, lambda E: E.tensor_scalar_mul(va, ps[3][0:1, 0:NC], 1.0 / D))
            p.op("vector", None, lambda E: E.tensor_mul(tr, mu, mu))
            c1 = p.op("vector", "sEV", lambda E: E.tensor_sub(va, va, tr))
            bank_guard[2] = ("sEV", c1)
            bank_guard[3] = ("sEV", c1)
            p.wait("scalar", "sEV", c1)
            c2 = p.op("scalar", "sSC", lambda E: E.activation(tr, va, AF.Sqrt, bias=P["epsc"][0:1, 0:1]))
            p.wait("vector", "sSC", c2)
            c3 = p.op("vector", "sEV", lambda E: E.reciprocal(va, tr))
            p.wait("tensor", "sEV", c3)
            for k, r in ((0, mu), (1, va)):
                bguard("tensor", 6)
                p.op("tensor", "sMM", lambda E, r=r: E.matmul(
                    ps[6][:, 0:NC], P["onesr"][:], r, start=True, stop=True))
                mm = p.cnt["sMM"]
                p.wait("scalar", "sMM", mm)
                c = p.op("scalar", "sSC", lambda E, k=k, n=n: E.copy(
                    C0["stat"][:, (2 * n + k) * NC:(2 * n + k) * NC + NC], ps[6][:, 0:NC]))
                bank_guard[6] = ("sSC", c)
            buf_guard[("sq", 0)] = ("sMM", p.cnt["sMM"])
            buf_guard[("sq", 1)] = ("sMM", p.cnt["sMM"])

    def ln_apply(getsrc, getdst, affine=None, chunk_cb=None):
        sc = p.cnt["sSC"]
        p.wait("vector", "sSC", sc)
        for n in range(NB):
            for pt in range(PD):
                mu = C0["stat"][:, (2 * n) * NC:(2 * n) * NC + NC]
                rs = C0["stat"][:, (2 * n + 1) * NC:(2 * n + 1) * NC + NC]
                src = getsrc(pt)[:, n * NC:(n + 1) * NC]
                dst = getdst(pt, n)
                tmp = C0["t0"][:, 0:NC]
                p.op("vector", None, lambda E, s=src, m=mu, t=tmp: E.tensor_sub(t, s, m))
                if affine is None:
                    p.op("vector", "sEV", lambda E, d=dst, r=rs, t=tmp: E.tensor_mul(d, t, r))
                else:
                    go, bo = affine
                    tmp2 = C0["t1"][:, 0:NC]
                    p.op("vector", None, lambda E, r=rs, t=tmp, t2=tmp2: E.tensor_mul(t2, t, r))
                    p.op("vector", "sEV", lambda E, d=dst, t2=tmp2, go=go, bo=bo, pt=pt:
                         E.tensor_scalar(out=d, in0=t2,
                                         scalar1=P["scol"][:, SC[go] + pt:SC[go] + pt + 1],
                                         scalar2=P["scol"][:, SC[bo] + pt:SC[bo] + pt + 1],
                                         op0=ALU.mult, op1=ALU.add))
            if chunk_cb:
                chunk_cb(n)

    def layer(R, w, nk, nm, ncols, rhs, evac, slabs=("sl0", "sl1"), banks=(0, 1),
              rhs_wait=None):
        if rhs_wait is not None:
            p.wait("tensor", rhs_wait[0], rhs_wait[1])
        for m in range(nm):
            buf = slabs[m % len(slabs)]
            uguard("sync", ("slab", id(R), buf))
            wv = p.dma("sync",
                       R[buf][:, 0:nk * 128],
                       stash[w][m * nk * 128 * 128:(m + 1) * nk * 128 * 128]
                       .rearrange("(p f) -> p f", p=128), "sW" + buf)
            p.wait("tensor", "sW" + buf, wv)
            bidx = banks[m % len(banks)]
            bguard("tensor", bidx)
            for k in range(nk):
                p.op("tensor", "sMM" if k == nk - 1 else None,
                     lambda E, b=bidx, bu=buf, k=k, f=(k == 0), l=(k == nk - 1):
                     E.matmul(ps[b][:, 0:ncols], R[bu][:, k * 128:(k + 1) * 128],
                              rhs(k), start=f, stop=l))
            evac(m, bidx, p.cnt["sMM"])
            buf_guard[("slab", id(R), buf)] = ("sMM", p.cnt["sMM"])

    def ffn_chunk(getx, getres, getout=None, out_dma=None):
        lnqb = lambda pt: R1["Gb"][:, pt * Q:(pt + 1) * Q]
        ln_stats(getx)
        ln_apply(getx, lambda pt, n: lnqb(pt)[:, n * NC:(n + 1) * NC])
        evq = p.cnt["sEV"]
        for m in range(MF):
            bufA = ("sl0", "sl1")[m % 2]
            bufB = ("sl2", "sl3")[m % 2]
            uguard("sync", ("slab", id(R1), bufA))
            uguard("sync", ("slab", id(R1), bufB))
            wva = p.dma("sync",
                        R1[bufA][:, 0:PD * 128],
                        stash["w1"][m * PD * 128 * 128:(m + 1) * PD * 128 * 128]
                        .rearrange("(p f) -> p f", p=128), "sW" + bufA)
            wvb = p.dma("sync",
                        R1[bufB][:, 0:PD * 128],
                        stash["w2"][m * PD * 128 * 128:(m + 1) * PD * 128 * 128]
                        .rearrange("(p f) -> p f", p=128), "sW" + bufB)
            p.wait("tensor", "sW" + bufA, wva)
            p.wait("tensor", "sW" + bufB, wvb)
            if m == 0:
                p.wait("tensor", "sEV", evq)
            bguard("tensor", 0)
            bguard("tensor", 1)
            for k in range(PD):
                p.op("tensor", None, lambda E, k=k, bu=bufA, f=(k == 0):
                     E.matmul(ps[0][:, 0:NC], R1[bu][:, k * 128:(k + 1) * 128],
                              lnqb(k)[:, 0:NC], start=f, stop=(k == PD - 1)))
            for k in range(PD):
                p.op("tensor", "sMM" if k == PD - 1 else None,
                     lambda E, k=k, bu=bufB, f=(k == 0):
                     E.matmul(ps[1][:, 0:NC], R1[bu][:, k * 128:(k + 1) * 128],
                              lnqb(k)[:, 0:NC], start=f, stop=(k == PD - 1)))
            mm = p.cnt["sMM"]
            sa = C0["t1"][:, (m % 2) * NC:(m % 2) * NC + NC]
            p.wait("scalar", "sMM", mm)
            uguard("scalar", ("t1", m % 2))
            c1 = p.op("scalar", "sSC", lambda E, sa=sa, m=m: E.activation(
                sa, ps[0][:, 0:NC], AF.Silu,
                scale=P["scol"][:, SC["s_w1"] + m:SC["s_w1"] + m + 1]))
            bank_guard[0] = ("sSC", c1)
            p.wait("vector", "sSC", c1)
            c2 = p.op("vector", "sEV", lambda E, sa=sa, m=m: E.scalar_tensor_tensor(
                out=R1["Hb"][:, m * Q:m * Q + NC], in0=ps[1][:, 0:NC],
                scalar=P["scol"][:, SC["s_w2"] + m:SC["s_w2"] + m + 1],
                in1=sa, op0=ALU.mult, op1=ALU.mult))
            bank_guard[1] = ("sEV", c2)
            buf_guard[("t1", m % 2)] = ("sEV", c2)
            buf_guard[("slab", id(R1), bufA)] = ("sMM", mm)
            buf_guard[("slab", id(R1), bufB)] = ("sMM", mm)

        def evac_gelu(dstbuf, scname):
            def ev(m, bidx, mm):
                p.wait("scalar", "sMM", mm)
                c = p.op("scalar", "sSC", lambda E, m=m, b=bidx: E.activation(
                    R1[dstbuf][:, m * Q:m * Q + NC], ps[b][:, 0:NC], AF.Gelu,
                    scale=P["scol"][:, SC[scname] + m:SC[scname] + m + 1]))
                bank_guard[bidx] = ("sSC", c)
            return ev
        hev = p.cnt["sEV"]
        layer(R1, "h1", MF, MF, NC, lambda k: R1["Hb"][:, k * Q:k * Q + NC],
              evac_gelu("Gb", "s_h1"), rhs_wait=("sEV", hev))
        gsc = p.cnt["sSC"]
        layer(R1, "h2", MF, MF, NC, lambda k: R1["Gb"][:, k * Q:k * Q + NC],
              evac_gelu("Hb", "s_h2"), rhs_wait=("sSC", gsc))

        def evac_w3(m, bidx, mm):
            dst = (getout or getres)(m)[:, 0:NC]
            res = getres(m)[:, 0:NC]
            p.wait("vector", "sMM", mm)
            c = p.op("vector", "sEV", lambda E, d=dst, r=res, m=m, b=bidx:
                     E.scalar_tensor_tensor(
                         out=d, in0=ps[b][:, 0:NC],
                         scalar=P["scol"][:, SC["hs3"] + m:SC["hs3"] + m + 1],
                         in1=r, op0=ALU.mult, op1=ALU.add))
            bank_guard[bidx] = ("sEV", c)
            if out_dma is not None:
                p.wait("gpsimd", "sEV", c)
                p.dma("gpsimd", out_dma(m), dst, "sOUT")
        layer(R1, "w3", MF, PD, NC, lambda k: R1["Hb"][:, k * Q:k * Q + NC], evac_w3,
              rhs_wait=("sSC", p.cnt["sSC"]))

    def phase1():
        for q in range(T // Q):
            col0 = q * Q
            getx = lambda pt: R1["xq"][:, pt * Q:(pt + 1) * Q]
            if q > 0:
                p.wait("gpsimd", "sSP")
            lv = None
            for pt in range(PD):
                lv = p.dma("gpsimd", getx(pt),
                           h["xt"][pt * 128:(pt + 1) * 128, col0:col0 + Q], "sLD")
            for e in ("scalar", "vector", "tensor"):
                p.wait(e, "sLD", lv)
            ffn_chunk(getx, getx)
            ev = p.cnt["sEV"]
            p.wait("gpsimd", "sEV", ev)
            for pt in range(PD):
                p.dma("gpsimd", x1_sp[pt * 128:(pt + 1) * 128, col0:col0 + Q],
                      getx(pt), "sSP")
            xo = lambda pt: R1["Gf"][:, pt * Q:(pt + 1) * Q]
            ln_stats(getx)
            ln_apply(getx, lambda pt, n: xo(pt)[:, n * NC:(n + 1) * NC])
            ev = p.cnt["sEV"]
            p.wait("gpsimd", "sEV", ev)
            for pt in range(PD):
                p.dma("gpsimd", xo_sp[pt * 128:(pt + 1) * 128, col0:col0 + Q],
                      xo(pt), "sSP")
            ln_stats(xo)

            def spill_h(n):
                ev = p.cnt["sEV"]
                p.wait("gpsimd", "sEV", ev)
                for pt in range(PD):
                    p.dma("gpsimd", hT_sp[pt * 128:(pt + 1) * 128,
                                          col0 + n * NC:col0 + (n + 1) * NC],
                          R1["stgb"][:, pt * NC:(pt + 1) * NC], "sSP")
            ln_apply(xo, lambda pt, n: R1["stgb"][:, pt * NC:(pt + 1) * NC],
                     affine=("ng", "nb"), chunk_cb=spill_h)
            p.wait("vector", "sSP")
        p.wait("gpsimd", "sSP")
        p.barrier()

    def phase2_prep():
        g = "gpsimd"
        p.wait("sync", "sDM2", p.cnt["sDM2"])
        p.wait("gpsimd", "sDM2", p.cnt["sDM2"])
        p.dma(g, R2["eselS"][:], h["esel"][:], "sLD")
        for di, sfx in ((0, "1"), (1, "2")):
            p.dma(g, R2["xpT"][:, di * 16 * 96:(di + 1) * 16 * 96], h["xp" + sfx][:], "sLD")
            p.dma(g, R2["dtwT"][0:DTR, di * DI:(di + 1) * DI], h["dtw" + sfx][:], "sLD")
        mrow = R2["ubuf"][0:1, 0:NC]
        for wc in range(T // NC):
            mv = p.dma(g, mrow, h["mask"][:, wc * NC:(wc + 1) * NC], "sLD")
            p.wait("tensor", "sLD", mv)
            bguard("tensor", 6)
            mm = p.op("tensor", "sMM", lambda E: E.matmul(
                ps[6][:, 0:NC], P["onesr"][:], mrow, start=True, stop=True))
            p.wait("gpsimd", "sMM", mm)
            p.wait("scalar", "sMM", mm)
            c = p.op("scalar", "sSC", lambda E, wc=wc: E.copy(
                R2["maskbc"][:, wc * NC:(wc + 1) * NC], ps[6][:, 0:NC]))
            bank_guard[6] = ("sSC", c)
        p.barrier()
    def mamba_branch(dir_):
        w_in = ("in1", "in2")[dir_]
        w_out = ("out1", "out2")[dir_]
        xpT = R2["xpT"][:, dir_ * 16 * 96:(dir_ + 1) * 16 * 96]
        p.op("vector", "sEV", lambda E: E.memset(P["state"][:], 0.0))
        p.op("vector", "sEV", lambda E: E.memset(P["halo"][:], 0.0))
        pend = [None]

        def emit_out_proj(oc0, ygv):
            p.wait("tensor", "sEV", ygv)

            def evac_out(m, bidx, mm):
                p.wait("scalar", "sMM", mm)
                uguard("scalar", ("ost", m % 2))
                cc = p.op("scalar", "sSC", lambda E, m=m, b=bidx: E.copy(
                    R2["ost"][:, (m % 2) * CH:(m % 2) * CH + CH], ps[b][:, 0:CH]))
                bank_guard[bidx] = ("sSC", cc)
                p.wait("gpsimd", "sSC", cc)
                dv = p.dma("gpsimd", mout[dir_][m * 128:(m + 1) * 128, oc0:oc0 + CH],
                           R2["ost"][:, (m % 2) * CH:(m % 2) * CH + CH],
                           "sOS%d" % (m % 2))
                buf_guard[("ost", m % 2)] = ("sOS%d" % (m % 2), dv)
            layer(R2, w_out, NDB, PD, CH,
                  lambda k: R2["ygb"][:, k * CH:(k + 1) * CH],
                  evac_out, banks=(6, 7))
            buf_guard["ygb"] = ("sMM", p.cnt["sMM"])

        for c in range(NCH):
            pos0 = c * CH
            ext = c >= EXT0
            srcc = (pos0, pos0 + CH) if dir_ == 0 else (T - pos0 - CH, T - pos0)
            uguard("gpsimd", "hwin")
            lv = None
            for pt in range(PD):
                lv = p.dma("gpsimd", R2["hwin"][:, pt * CH:(pt + 1) * CH],
                           hT_sp[pt * 128:(pt + 1) * 128, srcc[0]:srcc[1]], "sHW")
            rhsbuf = R2["hwin"]
            if dir_ == 1:
                p.wait("vector", "sHW", lv)
                uguard("vector", "hrev")
                for pt in range(PD):
                    p.op("vector", "sEV" if pt == PD - 1 else None,
                         lambda E, pt=pt: E.tensor_copy(
                             R2["hrev"][:, pt * CH:(pt + 1) * CH],
                             R2["hwin"][:, pt * CH:(pt + 1) * CH][:, ::-1]))
                buf_guard["hwin"] = ("sEV", p.cnt["sEV"])
                rhsbuf = R2["hrev"]
                p.wait("tensor", "sEV")
            else:
                p.wait("tensor", "sHW", lv)

            def evac_in(m, bidx, mm):
                p.wait("scalar", "sMM", mm)
                if m < NDB:
                    cc = p.op("scalar", "sSC", lambda E, m=m, b=bidx: E.copy(
                        R2["xspre"][:, m * (CH + 4) + 3:m * (CH + 4) + 3 + CH],
                        ps[b][:, 0:CH]))
                    bank_guard[bidx] = ("sSC", cc)
                else:
                    uguard("scalar", ("zst", m % 2))
                    cc = p.op("scalar", "sSC", lambda E, m=m, b=bidx: E.copy(
                        R2["zst"][:, (m % 2) * CH:(m % 2) * CH + CH], ps[b][:, 0:CH]))
                    bank_guard[bidx] = ("sSC", cc)
                    p.wait("gpsimd", "sSC", cc)
                    dv = p.dma("gpsimd",
                               z_sp[dir_ * DI + (m - NDB) * 128:
                                    dir_ * DI + (m - NDB) * 128 + 128, pos0:pos0 + CH],
                               R2["zst"][:, (m % 2) * CH:(m % 2) * CH + CH],
                               "sZS%d" % (m % 2))
                    buf_guard[("zst", m % 2)] = ("sZS%d" % (m % 2), dv)
            p.wait("scalar", "sEV", p.cnt["sEV"])
            layer(R2, w_in, PD, 2 * NDB, CH,
                  lambda k: rhsbuf[:, k * CH:(k + 1) * CH], evac_in,
                  banks=(0, 1, 2, 3))
            inp_mm = p.cnt["sMM"]

            if pend[0] is not None:
                emit_out_proj(*pend[0])
                pend[0] = None

            evsc = p.cnt["sSC"]
            p.wait("vector", "sSC", evsc)
            for db in range(NDB):
                p.op("vector", None, lambda E, db=db: E.tensor_copy(
                    R2["xspre"][:, db * (CH + 4):db * (CH + 4) + 3],
                    P["halo"][:, db * 4:db * 4 + 3]))
                p.op("vector", None, lambda E, db=db: E.tensor_mul(
                    R2["xspre"][:, db * (CH + 4) + 3:db * (CH + 4) + 3 + CH],
                    R2["xspre"][:, db * (CH + 4) + 3:db * (CH + 4) + 3 + CH],
                    R2["maskbc"][:, pos0:pos0 + CH]))
            for db in range(NDB):
                seg = R2["xspre"][:, db * (CH + 4):db * (CH + 4) + CH + 4]
                acc = R2["xsconv"][:, db * CH:(db + 1) * CH]
                p.op("vector", None, lambda E, s=seg, a=acc, db=db: E.tensor_scalar(
                    out=a, in0=s[:, 0:CH],
                    scalar1=P["cwcol"][:, dir_ * NDB * 4 + db * 4:dir_ * NDB * 4 + db * 4 + 1],
                    scalar2=None, op0=ALU.mult))
                for k in range(1, 4):
                    p.op("vector", "sEV" if k == 3 else None,
                         lambda E, s=seg, a=acc, db=db, k=k: E.scalar_tensor_tensor(
                             out=a, in0=s[:, k:k + CH],
                             scalar=P["cwcol"][:, dir_ * NDB * 4 + db * 4 + k:
                                               dir_ * NDB * 4 + db * 4 + k + 1],
                             in1=a, op0=ALU.mult, op1=ALU.add))
                ev = p.cnt["sEV"]
                p.wait("scalar", "sEV", ev)
                p.op("scalar", "sSC", lambda E, a=acc, db=db: E.activation(
                    a, a, AF.Silu,
                    bias=P["cbcol"][:, dir_ * NDB + db:dir_ * NDB + db + 1]))
            convsc = p.cnt["sSC"]
            for db in range(NDB):
                p.op("vector", None, lambda E, db=db: E.tensor_copy(
                    P["halo"][:, db * 4:db * 4 + 3],
                    R2["xspre"][:, db * (CH + 4) + CH:db * (CH + 4) + CH + 3]))
            p.op("vector", "sEV", lambda E: E.engine_nop())

            p.wait("vector", "sSC", convsc)
            uguard("vector", "xsbf")
            for db in range(NDB):
                p.op("vector", "sEV" if db == NDB - 1 else None,
                     lambda E, db=db: E.tensor_copy(
                         R2["xsbf"][:, db * CH:(db + 1) * CH],
                         R2["xsconv"][:, db * CH:(db + 1) * CH]))
            xbv = p.cnt["sEV"]
            p.wait("tensor", "sEV", xbv)
            bguard("tensor", 4)
            for k in range(NDB):
                p.op("tensor", "sMM" if k == NDB - 1 else None,
                     lambda E, k=k, f=(k == 0), l=(k == NDB - 1): E.matmul(
                         ps[4][0:80, 0:CH], xpT[:, k * 96:k * 96 + 80],
                         R2["xsbf"][:, k * CH:(k + 1) * CH], start=f, stop=l))
            if ext:
                bguard("tensor", 5)
                for k in range(NDB):
                    p.op("tensor", "sMM" if k == NDB - 1 else None,
                         lambda E, k=k, f=(k == 0), l=(k == NDB - 1): E.matmul(
                             ps[5][0:16, 0:CH], xpT[:, k * 96 + 80:k * 96 + 96],
                             R2["xsbf"][:, k * CH:(k + 1) * CH], start=f, stop=l))
            mm2 = p.cnt["sMM"]
            p.wait("scalar", "sMM", mm2)
            p.op("scalar", "sSC", lambda E: E.copy(
                R2["dbl"][0:80, 0:CH], ps[4][0:80, 0:CH]))
            c1 = p.op("scalar", "sSC", lambda E: E.copy(
                R2["brow"][0:DS, 0:CH], ps[4][64:80, 0:CH]))
            bank_guard[4] = ("sSC", c1)
            if ext:
                c2 = p.op("scalar", "sSC", lambda E: E.copy(
                    R2["crow"][0:16, 0:CH], ps[5][0:16, 0:CH]))
                bank_guard[5] = ("sSC", c2)
            sc_dbl = p.cnt["sSC"]
            p.wait("vector", "sSC", sc_dbl)
            dtc = p.op("vector", "sEV", lambda E: E.tensor_copy(
                R2["dblbf"][0:DTR, 0:CH], R2["dbl"][0:DTR, 0:CH]))
            p.wait("tensor", "sEV", dtc)

            for db in range(NDB):
                bguard("tensor", 4)
                mm = p.op("tensor", "sMM", lambda E, db=db: E.matmul(
                    ps[4][:, 0:CH],
                    R2["dtwT"][0:DTR, dir_ * DI + db * 128:dir_ * DI + (db + 1) * 128],
                    R2["dblbf"][0:DTR, 0:CH], start=True, stop=True))
                p.wait("vector", "sMM", mm)
                cc = p.op("vector", "sEV", lambda E, db=db: E.tensor_scalar(
                    out=R2["dtb"][:, db * CH:(db + 1) * CH], in0=ps[4][:, 0:CH],
                    scalar1=P["dtbcol"][:, dir_ * NDB + db:dir_ * NDB + db + 1],
                    scalar2=85.0, op0=ALU.add, op1=ALU.min))
                bank_guard[4] = ("sEV", cc)
            dtev0 = p.cnt["sEV"]
            p.wait("scalar", "sEV", dtev0)
            p.op("scalar", None, lambda E: E.activation(
                R2["ubuf"][:], R2["dtb"][:], AF.Sigmoid, scale=P["negcol"][:, 0:1]))
            lnc = p.op("scalar", "sSC", lambda E: E.activation(
                R2["ubuf"][:], R2["ubuf"][:], AF.Ln))
            p.wait("vector", "sSC", lnc)
            mb3 = R2["maskbc"][:, pos0:pos0 + CH][:, None, :].broadcast_to([128, NDB, CH])
            p.op("vector", None, lambda E: E.scalar_tensor_tensor(
                out=R2["dtb"].rearrange("p (d t) -> p d t", d=NDB),
                in0=R2["ubuf"].rearrange("p (d t) -> p d t", d=NDB),
                scalar=P["negcol"][:, 0:1], in1=mb3, op0=ALU.mult, op1=ALU.mult))
            p.op("vector", "sEV", lambda E: E.tensor_mul(
                R2["ubuf"][:], R2["dtb"][:], R2["xsconv"][:]))
            dt_ev = p.cnt["sEV"]

            PL = SGRP * CH

            def dap(par):
                return R2["daP"][:, par * PL:(par + 1) * PL]

            def dbxp(par):
                return R2["dbxP"][:, par * PL:(par + 1) * PL]

            def hp(par):
                return R2["hP"][:, par * PL:(par + 1) * PL]
            bbc3 = R2["bbcP"].rearrange("p (s t) -> p s t", s=SGRP)
            for g in range(NG):
                p.wait("tensor", "sSC", sc_dbl)
                uguard("tensor", ("bbcP",))
                for si in range(SGRP):
                    s = g * SGRP + si
                    bguard("tensor", 5)
                    mm = p.op("tensor", "sMM", lambda E, s=s: E.matmul(
                        ps[5][:, 0:CH], R2["eselS"][:, s * 128:(s + 1) * 128],
                        R2["brow"][0:DS, 0:CH], start=True, stop=True))
                    p.wait("scalar", "sMM", mm)
                    cb = p.op("scalar", "sSC", lambda E, si=si: E.copy(
                        R2["bbcP"][:, si * CH:(si + 1) * CH], ps[5][:, 0:CH]))
                    bank_guard[5] = ("sSC", cb)
                    if ext:
                        bguard("tensor", 6)
                        mm = p.op("tensor", "sMM", lambda E, s=s: E.matmul(
                            ps[6][:, 0:CH], R2["eselS"][:, s * 128:(s + 1) * 128],
                            R2["crow"][0:16, 0:CH], start=True, stop=True))
                        p.wait("scalar", "sMM", mm)
                        ccp = p.op("scalar", "sSC", lambda E, si=si: E.copy(
                            R2["cbcP"][:, si * CH:(si + 1) * CH], ps[6][:, 0:CH]))
                        bank_guard[6] = ("sSC", ccp)
                bc_sc = p.cnt["sSC"]
                for db in range(NDB):
                    par = db % 2
                    da = dap(par)
                    dbx = dbxp(par)
                    hh = hp(par)
                    pp = R2["pP"][:, par * PL:(par + 1) * PL]
                    p.wait("scalar", "sEV", dt_ev)
                    uguard("scalar", ("daP", par))
                    for si in range(SGRP):
                        s = g * SGRP + si
                        p.op("scalar", "sSC" if si == SGRP - 1 else None,
                             lambda E, db=db, s=s, si=si, da=da: E.activation(
                                 da[:, si * CH:(si + 1) * CH],
                                 R2["dtb"][:, db * CH:(db + 1) * CH], AF.Exp,
                                 scale=P["acol"][:, dir_ * NDB * DS + db * DS + s:
                                                 dir_ * NDB * DS + db * DS + s + 1]))
                    da_sc = p.cnt["sSC"]
                    p.wait("vector", "sSC", bc_sc)
                    uguard("vector", ("dbxP", par))
                    p.op("vector", None, lambda E, db=db, dbx=dbx: E.tensor_mul(
                        dbx.rearrange("p (s t) -> p s t", s=SGRP),
                        R2["ubuf"][:, db * CH:(db + 1) * CH][:, None, :]
                        .broadcast_to([128, SGRP, CH]),
                        bbc3))
                    p.wait("vector", "sSC", da_sc)
                    uguard("vector", ("hP", par))
                    for si in range(SGRP):
                        s = g * SGRP + si
                        p.op("vector", "sEV" if si == SGRP - 1 else None,
                             lambda E, db=db, s=s, si=si, da=da, dbx=dbx, hh=hh:
                             E.tensor_tensor_scan(
                                 hh[:, si * CH:(si + 1) * CH],
                                 da[:, si * CH:(si + 1) * CH],
                                 dbx[:, si * CH:(si + 1) * CH],
                                 P["state"][:, db * DS + s:db * DS + s + 1],
                                 ALU.mult, ALU.add))
                    scan_ev = p.cnt["sEV"]
                    buf_guard[("daP", par)] = ("sEV", scan_ev + 1)
                    stv = p.op("vector", "sEV", lambda E, db=db, g=g, hh=hh: E.tensor_copy(
                        P["state"][:, db * DS + g * SGRP:db * DS + (g + 1) * SGRP],
                        hh[:, CH - 1:SGRP * CH:CH]))
                    if ext:
                        p.wait("gpsimd", "sEV", stv)
                        p.wait("gpsimd", "sSC", bc_sc)
                        uguard("gpsimd", ("pP", par))
                        p.op("gpsimd", None, lambda E, hh=hh, pp=pp: E.tensor_mul(
                            pp, hh, R2["cbcP"]))
                        p.op("gpsimd", None, lambda E, pp=pp: E.tensor_add(
                            pp[:, 0:CH], pp[:, 0:CH], pp[:, CH:2 * CH]))
                        p.op("gpsimd", None, lambda E, pp=pp: E.tensor_add(
                            pp[:, 2 * CH:3 * CH], pp[:, 2 * CH:3 * CH],
                            pp[:, 3 * CH:4 * CH]))
                        if g == 0:
                            rv = p.op("gpsimd", "sGP", lambda E, db=db, pp=pp: E.tensor_add(
                                R2["yb"][:, db * CH:(db + 1) * CH],
                                pp[:, 0:CH], pp[:, 2 * CH:3 * CH]))
                        else:
                            p.op("gpsimd", None, lambda E, pp=pp: E.tensor_add(
                                pp[:, 0:CH], pp[:, 0:CH], pp[:, 2 * CH:3 * CH]))
                            rv = p.op("gpsimd", "sGP", lambda E, db=db, pp=pp: E.tensor_add(
                                R2["yb"][:, db * CH:(db + 1) * CH],
                                R2["yb"][:, db * CH:(db + 1) * CH], pp[:, 0:CH]))
                        buf_guard[("hP", par)] = ("sGP", rv)
                        buf_guard[("pP", par)] = ("sGP", rv)
                        buf_guard[("dbxP", par)] = ("sGP", rv)
                    else:
                        buf_guard[("hP", par)] = ("sEV", stv)
                        buf_guard[("dbxP", par)] = ("sEV", stv)
                buf_guard[("bbcP",)] = ("sEV", p.cnt["sEV"])

            if ext:
                p.wait("vector", "sGP", p.cnt["sGP"])
                p.wait("gpsimd", "sZS0", p.cnt["sZS0"])
                p.wait("gpsimd", "sZS1", p.cnt["sZS1"])
                p.wait("gpsimd", "sEV", p.cnt["sEV"])
                for db in range(NDB):
                    uguard("gpsimd", ("zrd", db % 2))
                    zv = p.dma("gpsimd", R2["zrd"][:, (db % 2) * CH:(db % 2) * CH + CH],
                               z_sp[dir_ * DI + db * 128:dir_ * DI + (db + 1) * 128,
                                    pos0:pos0 + CH], "sZR%d" % (db % 2))
                    p.op("vector", None, lambda E, db=db: E.scalar_tensor_tensor(
                        out=R2["yb"][:, db * CH:(db + 1) * CH],
                        in0=R2["xsconv"][:, db * CH:(db + 1) * CH],
                        scalar=P["dpcol"][:, dir_ * NDB + db:dir_ * NDB + db + 1],
                        in1=R2["yb"][:, db * CH:(db + 1) * CH],
                        op0=ALU.mult, op1=ALU.add))
                    p.wait("scalar", "sZR%d" % (db % 2), zv)
                    uguard("scalar", ("szb",))
                    szc = p.op("scalar", "sSC", lambda E, db=db: E.activation(
                        R2["szb"][:, 0:CH], R2["zrd"][:, (db % 2) * CH:(db % 2) * CH + CH],
                        AF.Silu))
                    buf_guard[("zrd", db % 2)] = ("sSC", szc)
                    buf_guard[("daP", 0)] = ("sSC", szc)
                    p.wait("vector", "sSC", szc)
                    uguard("vector", "ygb")
                    yv = p.op("vector", "sEV", lambda E, db=db: E.tensor_mul(
                        R2["ygb"][:, db * CH:(db + 1) * CH],
                        R2["yb"][:, db * CH:(db + 1) * CH], R2["szb"][:, 0:CH]))
                    buf_guard[("szb",)] = ("sEV", yv)
                pend[0] = ((c - EXT0) * CH, p.cnt["sEV"])
            buf_guard["hwin"] = ("sMM", inp_mm)
            buf_guard["hrev"] = ("sMM", inp_mm)
            buf_guard["xsbf"] = ("sMM", mm2)
        if pend[0] is not None:
            emit_out_proj(*pend[0])
            pend[0] = None


    def phase3():
        for q in range(OWN // Q):
            bc0 = T // 2 + q * Q
            oc0 = q * Q
            Afn = lambda pt: R1["Hf"][:, pt * Q:(pt + 1) * Q]
            Bfn = lambda pt: R1["Hf"][:, (PD + pt) * Q:(PD + pt + 1) * Q]
            xofn = lambda pt: R1["Gf"][:, pt * Q:(pt + 1) * Q]
            x2fn = lambda pt: R1["Gf"][:, (PD + pt) * Q:(PD + pt + 1) * Q]
            x1fn = lambda pt: R1["xq"][:, pt * Q:(pt + 1) * Q]
            p.wait("gpsimd", "sEV")
            p.wait("gpsimd", "sOUT")
            lv = None
            for pt in range(PD):
                p.dma("gpsimd", Afn(pt), mout[0][pt * 128:(pt + 1) * 128, oc0:oc0 + Q], "sLD")
                p.dma("gpsimd", Bfn(pt), mout[1][pt * 128:(pt + 1) * 128, oc0:oc0 + Q], "sLD")
                p.dma("gpsimd", xofn(pt), xo_sp[pt * 128:(pt + 1) * 128, bc0:bc0 + Q], "sLD")
                lv = p.dma("gpsimd", x1fn(pt), x1_sp[pt * 128:(pt + 1) * 128, bc0:bc0 + Q], "sLD")
            p.wait("vector", "sLD", lv)
            for pt in range(PD):
                p.op("vector", None, lambda E, pt=pt: E.tensor_add(
                    Afn(pt)[:], Afn(pt)[:], Bfn(pt)[:]))
                p.op("vector", "sEV", lambda E, pt=pt: E.tensor_add(
                    Afn(pt)[:], Afn(pt)[:], xofn(pt)[:]))
            ev = p.cnt["sEV"]
            p.wait("scalar", "sEV", ev)
            for pt in range(PD):
                p.op("scalar", "sSC", lambda E, pt=pt: E.activation(
                    Afn(pt)[:], Afn(pt)[:], AF.Gelu))
            sc = p.cnt["sSC"]
            p.wait("vector", "sSC", sc)
            spv = None
            for pt in range(PD):
                cv = p.op("vector", "sEV", lambda E, pt=pt: E.tensor_add(
                    x2fn(pt)[:], xofn(pt)[:], Afn(pt)[:]))
                p.wait("gpsimd", "sEV", cv)
                spv = p.dma("gpsimd", x2_sp[pt * 128:(pt + 1) * 128, oc0:oc0 + Q],
                            x2fn(pt), "sSP")
            ln_stats(x2fn)
            hfb = lambda pt: R1["Hb"][:, pt * Q:(pt + 1) * Q]
            ln_apply(x2fn, lambda pt, n: hfb(pt)[:, n * NC:(n + 1) * NC],
                     affine=("fg", "fb"))
            ev = p.cnt["sEV"]
            p.wait("tensor", "sEV", ev)

            def evac_ff1(m, bidx, mm):
                p.wait("scalar", "sMM", mm)
                cc = p.op("scalar", "sSC", lambda E, m=m, b=bidx: E.activation(
                    R1["Gb"][:, m * Q:m * Q + NC], ps[b][:, 0:NC], AF.Gelu,
                    bias=P["scol"][:, SC["ff1b"] + m:SC["ff1b"] + m + 1]))
                bank_guard[bidx] = ("sSC", cc)
            p.wait("scalar", "sSP", spv)
            layer(R1, "ff1", PD, MF, NC, lambda k: hfb(k)[:, 0:NC], evac_ff1)

            p.wait("gpsimd", "sSP", spv)

            def evac_ff2(m, bidx, mm):
                uguard("gpsimd", ("stg", m % 2))
                lx = p.dma("gpsimd", R1["stg"][:, (m % 2) * NC:(m % 2) * NC + NC],
                           x2_sp[m * 128:(m + 1) * 128, oc0:oc0 + NC], "sST%d" % (m % 2))
                p.wait("vector", "sST%d" % (m % 2), lx)
                p.wait("vector", "sMM", mm)
                p.op("vector", None, lambda E, m=m, b=bidx: E.scalar_tensor_tensor(
                    out=C0["t0"][:, 0:NC], in0=ps[b][:, 0:NC],
                    scalar=P["scol"][:, SC["ff2b"] + m:SC["ff2b"] + m + 1],
                    in1=R1["stg"][:, (m % 2) * NC:(m % 2) * NC + NC],
                    op0=ALU.add, op1=ALU.add))
                cc = p.op("vector", "sEV", lambda E, m=m: E.tensor_add(
                    x1fn(m)[:, 0:NC], x1fn(m)[:, 0:NC], C0["t0"][:, 0:NC]))
                bank_guard[bidx] = ("sEV", cc)
                buf_guard[("stg", m % 2)] = ("sEV", cc)
            layer(R1, "ff2", MF, PD, NC, lambda k: R1["Gb"][:, k * Q:k * Q + NC],
                  evac_ff2, rhs_wait=("sSC", p.cnt["sSC"]))

            ffn_chunk(x1fn, x1fn,
                      out_dma=lambda m: out_h[m * 128:(m + 1) * 128, oc0:oc0 + Q])
        p.wait("gpsimd", "sOUT")
        p.barrier()

    phase0()
    if _PHASES >= 1:
        phase1()
    if _PHASES >= 2:
        phase2_prep()
        mamba_branch(0)
        p.barrier()
        mamba_branch(1)
        p.barrier()
    if _PHASES >= 3:
        phase3()

    with nc.Block() as block:
        for name, dec in (("sync", block.sync), ("tensor", block.tensor),
                          ("vector", block.vector), ("scalar", block.scalar),
                          ("gpsimd", block.gpsimd)):
            def make(name):
                def emitall(E):
                    for fn in p.q[name]:
                        fn(E)
                return emitall
            dec(make(name))
    return nc



def _tile_w(W):
    O, I = W.shape
    Wt = np.ascontiguousarray(np.asarray(W, np.float32).T)
    a = Wt.reshape(I // 128, 128, O // 128, 128).transpose(2, 1, 0, 3)
    return np.ascontiguousarray(a).reshape(-1)


def _cols(v):
    v = np.asarray(v, np.float32).reshape(-1)
    n = v.size // 128
    return np.ascontiguousarray(v.reshape(n, 128).T)


def _cols2(M):
    M = np.asarray(M, np.float32)
    dI, k = M.shape
    return np.ascontiguousarray(M.reshape(NDB, 128, k).transpose(1, 0, 2).reshape(128, NDB * k))


_WCACHE = {}


def make_weight_feed(d):
    key = id(d)
    if key in _WCACHE:
        return _WCACHE[key]
    feed = {}
    es = np.zeros((DS, DS * 128), np.float32)
    for s in range(DS):
        es[s, s * 128:(s + 1) * 128] = 1.0
    feed["esel"] = es
    wmap = {
        "w1": d["L0_ffn_w1_w"], "w2": d["L0_ffn_w2_w"], "h1": d["L0_ffn_h1_w"],
        "h2": d["L0_ffn_h2_w"], "w3": d["L0_ffn_w3_w"], "ff1": d["L0_attn_ff1_w"],
        "ff2": d["L0_attn_ff2_w"], "in1": d["L0_attn_m1_in_w"],
        "in2": d["L0_attn_m2_in_w"], "out1": d["L0_attn_m1_out_w"],
        "out2": d["L0_attn_m2_out_w"],
    }
    for k, W in wmap.items():
        feed[f"w_{k}"] = _tile_w(W)
    for di, sfx in ((1, "1"), (2, "2")):
        xp = np.asarray(d[f"L0_attn_m{di}_xproj_w"], np.float32)
        xpT = np.ascontiguousarray(xp.T)
        feed[f"xp{sfx}"] = np.ascontiguousarray(
            xpT.reshape(16, 128, 96).transpose(1, 0, 2).reshape(128, 16 * 96))
        feed[f"dtw{sfx}"] = np.ascontiguousarray(
            np.asarray(d[f"L0_attn_m{di}_dt_w"], np.float32).T)
        feed[f"alog{sfx}"] = _cols2(d[f"L0_attn_m{di}_Alog"])
        feed[f"cw{sfx}"] = _cols2(d[f"L0_attn_m{di}_conv_w"])
        feed[f"cb{sfx}"] = _cols(d[f"L0_attn_m{di}_conv_b"])
        feed[f"dtb{sfx}"] = _cols(d[f"L0_attn_m{di}_dt_b"])
        feed[f"dp{sfx}"] = _cols(d[f"L0_attn_m{di}_D"])
    feed["s_w1"] = _cols(d["L0_ffn_w1_s"])
    feed["s_w2"] = _cols(d["L0_ffn_w2_s"])
    feed["s_h1"] = _cols(d["L0_ffn_h1_s"])
    feed["s_h2"] = _cols(d["L0_ffn_h2_s"])
    feed["s_w3"] = _cols(d["L0_ffn_w3_s"])
    feed["ng"] = _cols(d["L0_attn_norm_g"])
    feed["nb"] = _cols(d["L0_attn_norm_b"])
    feed["fg"] = _cols(d["L0_attn_ffn_g"])
    feed["fb"] = _cols(d["L0_attn_ffn_b"])
    feed["ff1b"] = _cols(d["L0_attn_ff1_b"])
    feed["ff2b"] = _cols(d["L0_attn_ff2_b"])
    _WCACHE[key] = feed
    return feed


def make_inputs(d, b, j, T=4096):
    x = np.asarray(d["L0_attn_x"][b], np.float32)
    if j == 0:
        x = np.concatenate([x[T // 2:], x[:T // 2]], 0)
    feed = {"xt": np.ascontiguousarray(x.T)}
    m = np.ones((1, T), np.float32)
    if j == 0:
        m[0, :T // 2] = 0.0
    feed["mask"] = m
    feed.update(make_weight_feed(d))
    return feed


def kernel(**inputs):
    T = 4096
    nc = build(T)
    in_maps = [make_inputs(inputs, c // 2, c % 2, T) for c in range(8)]
    res = run_bass_kernel_spmd(nc, in_maps, core_ids=list(range(8)))
    B, L = inputs["L0_attn_x"].shape[0], inputs["L0_attn_x"].shape[1]
    out = np.zeros((B, L, D), np.float32)
    for c in range(8):
        b, j = c // 2, c % 2
        o = np.asarray(res.results[c]["out"])
        out[b, j * (L // 2):(j + 1) * (L // 2)] = o.T
    return out


# version 11
# speedup vs baseline: 1.0985x; 1.0985x over previous
import numpy as np
import ml_dtypes
import concourse.bass as bass
import concourse.mybir as mybir
from concourse.bass_utils import run_bass_kernel_spmd

F32 = mybir.dt.float32
BF16 = mybir.dt.bfloat16
AF = mybir.ActivationFunctionType
ALU = mybir.AluOpType
AX = mybir.AxisListType

D, DI, DS, DTR, DFF = 1024, 2048, 16, 64, 4096
PD, NDB, MF = D // 128, DI // 128, DFF // 128
ENGS = ("sync", "tensor", "vector", "scalar", "gpsimd")

BIGW = {
    "w1": (DFF, D), "w2": (DFF, D), "h1": (DFF, DFF), "h2": (DFF, DFF),
    "w3": (D, DFF), "ff1": (DFF, D), "ff2": (D, DFF),
    "in1": (2 * DI, D), "in2": (2 * DI, D), "out1": (D, DI), "out2": (D, DI),
}
SMALLW = [("alog1", DS * NDB), ("alog2", DS * NDB), ("cw1", 4 * NDB),
          ("cw2", 4 * NDB), ("cb1", NDB), ("cb2", NDB), ("dtb1", NDB),
          ("dtb2", NDB), ("dp1", NDB), ("dp2", NDB), ("s_w1", MF),
          ("s_w2", MF), ("s_h1", MF), ("s_h2", MF), ("s_w3", PD),
          ("ng", PD), ("nb", PD), ("fg", PD), ("fb", PD),
          ("ff1b", MF), ("ff2b", PD)]
SCW = dict(SMALLW)

_DEV = 0
_PHASES = 99


class Plan:
    def __init__(self, nc):
        self.nc = nc
        self.q = {e: [] for e in ENGS}
        self.cnt = {}
        self.sems = {}

    def sem(self, name):
        self.sems[name] = self.nc.semaphore(name).__enter__()
        self.cnt[name] = 0

    def emit(self, eng, fn):
        self.q[eng].append(fn)

    def wait(self, eng, name, val=None):
        v = self.cnt[name] if val is None else val
        if v <= 0:
            return
        s = self.sems[name]
        self.emit(eng, lambda E, s=s, v=v: E.wait_ge(s, v))

    def dma(self, eng, out, in_, sem="sDM", **kw):
        s = self.sems[sem]
        self.cnt[sem] += 16
        self.emit(eng, lambda E, o=out, i=in_, s=s, kw=kw:
                  E.dma_start(out=o, in_=i, **kw).then_inc(s, 16))
        return self.cnt[sem]

    ENGSEM = {"scalar": "sSC", "vector": "sEV", "gpsimd": "sGP"}

    def op(self, eng, sem, fn):
        auto = self.ENGSEM.get(eng)
        if auto is not None:
            assert sem is None or sem == auto, (eng, sem)
            if eng == "scalar":
                prev = self.cnt[auto]
                if prev > 0:
                    sa = self.sems[auto]
                    self.emit(eng, lambda E, s=sa, v=prev: E.wait_ge(s, v))
            sem = auto
        s = self.sems[sem] if sem else None
        if sem:
            self.cnt[sem] += 1

        def run(E, fn=fn, s=s):
            inst = fn(E)
            if s is not None:
                inst.then_inc(s, 1)
        self.emit(eng, run)
        return self.cnt[sem] if sem else None

    GPD = ("sLD", "sSP", "sHW", "sZR0", "sZR1", "sZS0", "sZS1",
           "sOS0", "sOS1", "sST0", "sST1", "sDM", "sOUT")
    SYD = ("sWsl0", "sWsl1", "sWsl2", "sWsl3")

    def barrier(self):
        s = self.sems["ph"]
        target = self.cnt["ph"] + len(ENGS)
        self.cnt["ph"] = target
        for e in ENGS:
            auto = self.ENGSEM.get(e)
            if e == "tensor":
                auto = "sMM"
            if auto is not None and self.cnt[auto] > 0:
                sa, va = self.sems[auto], self.cnt[auto]
                self.emit(e, lambda E, s=sa, v=va: E.wait_ge(s, v))
            drains = self.GPD if e == "gpsimd" else (self.SYD if e == "sync" else ())
            for dn in drains:
                if self.cnt.get(dn, 0) > 0:
                    self.wait(e, dn, self.cnt[dn])
            self.emit(e, lambda E, s=s: E.nop().then_inc(s, 1))
            self.emit(e, lambda E, s=s, t=target: E.wait_ge(s, t))


def build(T=4096):
    CH = T // 16
    NCH = 16
    EXT0 = 8
    Q = T // 8
    OWN = T // 2
    NC = min(512, Q)
    NB = Q // NC
    assert NB == 1 and CH <= 512

    nc = bass.Bass("TRN2", target_bir_lowering=False, debug=False)
    p = Plan(nc)
    for s in ["ph", "sMM", "sEV", "sSC", "sGP", "sDM", "sLD", "sSP", "sOUT",
              "sWsl0", "sWsl1", "sWsl2", "sWsl3", "sHW",
              "sZR0", "sZR1", "sZS0", "sZS1", "sOS0", "sOS1", "sST0", "sST1",
              "sDM2"]:
        p.sem(s)

    h = {}
    h["xt"] = nc.dram_tensor("xt", [D, T], F32, kind="ExternalInput").ap()
    h["mask"] = nc.dram_tensor("mask", [1, T], F32, kind="ExternalInput").ap()
    h["esel"] = nc.dram_tensor("esel", [DS, DS * 128], F32, kind="ExternalInput").ap()
    for w, (o, i) in BIGW.items():
        h[w] = nc.dram_tensor(f"w_{w}", [(o // 128) * (i // 128) * 128 * 128], F32,
                              kind="ExternalInput").ap()
    h["xp1"] = nc.dram_tensor("xp1", [128, 16 * 96], F32, kind="ExternalInput").ap()
    h["xp2"] = nc.dram_tensor("xp2", [128, 16 * 96], F32, kind="ExternalInput").ap()
    h["dtw1"] = nc.dram_tensor("dtw1", [DTR, DI], F32, kind="ExternalInput").ap()
    h["dtw2"] = nc.dram_tensor("dtw2", [DTR, DI], F32, kind="ExternalInput").ap()
    for nm, wdt in SMALLW:
        h[nm] = nc.dram_tensor(nm, [128, wdt], F32, kind="ExternalInput").ap()

    out_h = nc.dram_tensor("out", [D, OWN], F32, kind="ExternalOutput").ap()
    stash = {}
    for w, (o, i) in BIGW.items():
        stash[w] = nc.dram_tensor(f"st_{w}", [(o // 128) * (i // 128) * 128 * 128],
                                  BF16, kind="Internal").ap()
    kd1 = "ExternalOutput" if _DEV in (1, 3) else "Internal"
    kd2 = "ExternalOutput" if _DEV in (2, 3) else "Internal"
    hT_sp = nc.dram_tensor("hT_sp", [D, T], BF16, kind=kd1).ap()
    x1_sp = nc.dram_tensor("x1_sp", [D, T], F32, kind=kd1).ap()
    xo_sp = nc.dram_tensor("xo_sp", [D, T], F32, kind=kd1).ap()
    z_sp = nc.dram_tensor("z_sp", [2 * DI, T], F32, kind="Internal").ap()
    mout = [nc.dram_tensor("moutA", [D, OWN], F32, kind=kd2).ap(),
            nc.dram_tensor("moutB", [D, OWN], F32, kind=kd2).ap()]
    x2_sp = nc.dram_tensor("x2_sp", [D, OWN], F32, kind="Internal").ap()
    if _DEV >= 2:
        dbg = {nm: nc.dram_tensor("dbg_" + nm, [DI, CH], F32, kind="ExternalOutput").ap()
               for nm in ("xs", "dt", "y", "u")}
        dbg["bc"] = nc.dram_tensor("dbg_bc", [2 * DS, CH], F32, kind="ExternalOutput").ap()
    else:
        dbg = None

    P = {}

    def alloc(name, shape, dtype):
        P[name] = nc.sbuf_tensor(name, shape, dtype).__enter__().ap()
        return P[name]

    SC = {}
    off = 0
    for nm in ["s_w1", "s_w2", "s_h1", "s_h2", "s_w3", "hs3", "ng", "nb",
               "fg", "fb", "ff1b", "ff2b"]:
        SC[nm] = off
        off += SCW.get(nm, PD)
    alloc("scol", [128, off], F32)
    alloc("onesr", [1, 128], F32)
    alloc("onesc", [128, 1], F32)
    alloc("negcol", [128, 1], F32)
    alloc("cwcol", [128, 2 * NDB * 4], F32)
    alloc("cbcol", [128, 2 * NDB], F32)
    alloc("dtbcol", [128, 2 * NDB], F32)
    alloc("dpcol", [128, 2 * NDB], F32)
    alloc("acol", [128, 2 * NDB * DS], F32)
    alloc("state", [128, NDB * DS], F32)
    alloc("halo", [128, NDB * 4], F32)
    alloc("epsc", [1, 1], F32)

    AW = 46080
    arena = nc.sbuf_tensor("arena", [128, AW], F32).__enter__().ap()
    arena_b = arena.bitcast(BF16)

    class Reg(dict):
        def __init__(self, base=0):
            super().__init__()
            self.off = base

        def add(self, name, words, dtype=F32, pdim=128):
            o = self.off
            self.off += words
            assert self.off <= AW, (name, self.off)
            self.at(name, o, words, dtype, pdim)
            return self[name]

        def at(self, name, o, words, dtype=F32, pdim=128):
            if dtype == F32:
                self[name] = arena[0:pdim, o:o + words]
            else:
                self[name] = arena_b[0:pdim, 2 * o:2 * (o + words)]
            return self[name]

    C0 = Reg(0)
    C0.add("stat", 2 * NB * NC)
    SQ0 = C0.off
    C0.add("sq", 3 * NC)
    C0.add("t0", 2 * NC)
    C0.add("t1", 2 * NC)
    CEND = C0.off

    def row(i):
        return arena[0:1, SQ0 + i * NC:SQ0 + (i + 1) * NC]

    R1 = Reg(CEND)
    HB0 = R1.off
    R1.add("Hb", MF * Q // 2, BF16)
    GB0 = R1.off
    R1.add("Gb", MF * Q // 2, BF16)
    R1.add("xq", PD * Q)
    R1.add("sl0", MF * 128 // 2, BF16)
    R1.add("sl1", MF * 128 // 2, BF16)
    R1.add("sl2", MF * 128 // 2, BF16)
    R1.add("sl3", MF * 128 // 2, BF16)
    R1.add("stg", 2 * NC)
    R1.add("stgb", PD * NC // 2, BF16)
    R1.at("Hf", HB0, MF * Q // 2)
    R1.at("Gf", GB0, MF * Q // 2)

    SGRP = 4
    NG = DS // SGRP
    R2 = Reg(0)
    R2.add("maskbc", T // 2, BF16)
    R2.add("hwin", PD * CH // 2, BF16)
    R2.add("hrev", PD * CH // 2, BF16)
    R2.add("xspre", NDB * (CH + 4))
    R2.add("xsconv", NDB * CH)
    R2.add("dtb", NDB * CH)
    R2.add("yb", NDB * CH)
    R2.add("ubuf", NDB * CH)
    R2.add("ygb", NDB * CH // 2, BF16)
    DAP0 = R2.off
    R2.add("daP", 2 * SGRP * CH)
    DBXP0 = R2.off
    R2.add("dbxP", 2 * SGRP * CH)
    R2.add("hP", 2 * SGRP * CH)
    R2.add("bbcP", SGRP * CH)
    R2.add("cbcP", SGRP * CH)
    R2.at("xsbf", DAP0, NDB * CH // 2, BF16)
    R2.at("pP", DBXP0, 2 * SGRP * CH)
    R2.at("zrd", DAP0, 2 * CH)
    R2.at("szb", DAP0 + 2 * CH, CH)
    R2.add("sl0", 16 * 128 // 2, BF16)
    R2.add("sl1", 16 * 128 // 2, BF16)
    R2.add("ost", 2 * CH)
    R2.add("zst", 2 * CH)
    R2.add("dbl", CH)
    R2.add("crow", CH, F32, DS)
    R2.add("brow", CH, F32, DS)
    R2.add("dblbf", CH // 2, BF16, DTR)
    R2.add("eselS", DS * 128, F32, DS)
    R2.add("xpT", 16 * 96, BF16)
    R2.add("dtwT", DI, BF16, DTR)
    assert R2.off <= AW, R2.off

    bank_guard = {}
    buf_guard = {}

    def bguard(eng, key):
        if key in bank_guard:
            s, c = bank_guard[key]
            p.wait(eng, s, c)

    def uguard(eng, key):
        if key in buf_guard:
            s, c = buf_guard[key]
            p.wait(eng, s, c)

    ps = [nc.psum_tensor(f"ps{i}", [128, 512], F32).__enter__().ap() for i in range(8)]

    def phase0():
        g = "gpsimd"
        for nm in SC:
            if nm == "hs3":
                continue
            p.dma(g, P["scol"][:, SC[nm]:SC[nm] + SCW[nm]], h[nm][:, 0:SCW[nm]], "sLD")
        for di, sfx in ((0, "1"), (1, "2")):
            p.dma(g, P["cwcol"][:, di * NDB * 4:(di + 1) * NDB * 4], h["cw" + sfx][:], "sLD")
            p.dma(g, P["cbcol"][:, di * NDB:(di + 1) * NDB], h["cb" + sfx][:], "sLD")
            p.dma(g, P["dtbcol"][:, di * NDB:(di + 1) * NDB], h["dtb" + sfx][:], "sLD")
            p.dma(g, P["dpcol"][:, di * NDB:(di + 1) * NDB], h["dp" + sfx][:], "sLD")
            p.dma(g, P["acol"][:, di * NDB * DS:(di + 1) * NDB * DS], h["alog" + sfx][:], "sLD")
        ld = p.cnt["sLD"]
        p.wait("vector", "sLD", ld)
        p.op("vector", "sEV", lambda E: E.memset(P["onesr"][:], 1.0))
        p.op("vector", "sEV", lambda E: E.memset(P["onesc"][:], 1.0))
        p.op("vector", "sEV", lambda E: E.memset(P["negcol"][:], -1.0))
        p.op("vector", "sEV", lambda E: E.memset(P["state"][:], 0.0))
        p.op("vector", "sEV", lambda E: E.memset(P["halo"][:], 0.0))
        p.op("vector", "sEV", lambda E: E.memset(P["epsc"][:], 1e-5))
        p.op("vector", "sEV", lambda E: E.tensor_scalar_mul(
            P["scol"][:, SC["hs3"]:SC["hs3"] + PD],
            P["scol"][:, SC["s_w3"]:SC["s_w3"] + PD], 0.5))
        ev = p.cnt["sEV"]
        p.wait("scalar", "sLD", ld)
        p.wait("scalar", "sEV", ev)
        p.op("scalar", "sSC", lambda E: E.activation(P["acol"][:], P["acol"][:], AF.Exp))
        p.op("scalar", "sSC", lambda E: E.mul(P["acol"][:], P["acol"][:], P["negcol"][:, 0:1]))
        CHK = 1 << 19
        ffnw = ("w1", "w2", "h1", "h2", "w3")
        for w in ffnw + tuple(k for k in BIGW if k not in ffnw):
            o, i = BIGW[w]
            n = (o // 128) * (i // 128) * 128 * 128
            sm = "sDM" if w in ffnw else "sDM2"
            for ofs in range(0, n, CHK):
                ln = min(CHK, n - ofs)
                p.dma(g, stash[w][ofs:ofs + ln], h[w][ofs:ofs + ln], sm)
        p.wait("sync", "sDM", p.cnt["sDM"])
        p.barrier()

    def ln_stats(getsrc):
        p.wait("scalar", "sEV", p.cnt["sEV"])
        p.wait("tensor", "sEV", p.cnt["sEV"])
        for n in range(NB):
            bguard("tensor", 2)
            bguard("tensor", 3)
            for pt in range(PD):
                src = getsrc(pt)[:, n * NC:(n + 1) * NC]
                sqb = C0["sq"][:, (pt % 2) * NC:(pt % 2) * NC + NC]
                uguard("scalar", ("sq", pt % 2))
                c = p.op("scalar", "sSC", lambda E, s=src, d=sqb: E.activation(d, s, AF.Square))
                p.wait("tensor", "sSC", c)
                p.op("tensor", None, lambda E, s=src, f=(pt == 0), l=(pt == PD - 1):
                     E.matmul(ps[2][0:1, 0:NC], P["onesc"][:], s, start=f, stop=l))
                m2 = p.op("tensor", "sMM", lambda E, d=sqb, f=(pt == 0), l=(pt == PD - 1):
                          E.matmul(ps[3][0:1, 0:NC], P["onesc"][:], d, start=f, stop=l))
                buf_guard[("sq", pt % 2)] = ("sMM", m2)
            mmv = p.cnt["sMM"]
            p.wait("vector", "sMM", mmv)
            mu, va, tr = row(0), row(1), row(2)
            p.op("vector", None, lambda E: E.tensor_scalar_mul(mu, ps[2][0:1, 0:NC], 1.0 / D))
            p.op("vector", None, lambda E: E.tensor_scalar_mul(va, ps[3][0:1, 0:NC], 1.0 / D))
            p.op("vector", None, lambda E: E.tensor_mul(tr, mu, mu))
            c1 = p.op("vector", "sEV", lambda E: E.tensor_sub(va, va, tr))
            bank_guard[2] = ("sEV", c1)
            bank_guard[3] = ("sEV", c1)
            p.wait("scalar", "sEV", c1)
            c2 = p.op("scalar", "sSC", lambda E: E.activation(tr, va, AF.Sqrt, bias=P["epsc"][0:1, 0:1]))
            p.wait("vector", "sSC", c2)
            c3 = p.op("vector", "sEV", lambda E: E.reciprocal(va, tr))
            p.wait("tensor", "sEV", c3)
            for k, r in ((0, mu), (1, va)):
                bguard("tensor", 6)
                p.op("tensor", "sMM", lambda E, r=r: E.matmul(
                    ps[6][:, 0:NC], P["onesr"][:], r, start=True, stop=True))
                mm = p.cnt["sMM"]
                p.wait("scalar", "sMM", mm)
                c = p.op("scalar", "sSC", lambda E, k=k, n=n: E.copy(
                    C0["stat"][:, (2 * n + k) * NC:(2 * n + k) * NC + NC], ps[6][:, 0:NC]))
                bank_guard[6] = ("sSC", c)
            buf_guard[("sq", 0)] = ("sMM", p.cnt["sMM"])
            buf_guard[("sq", 1)] = ("sMM", p.cnt["sMM"])

    def ln_apply(getsrc, getdst, affine=None, chunk_cb=None):
        sc = p.cnt["sSC"]
        p.wait("vector", "sSC", sc)
        for n in range(NB):
            for pt in range(PD):
                mu = C0["stat"][:, (2 * n) * NC:(2 * n) * NC + NC]
                rs = C0["stat"][:, (2 * n + 1) * NC:(2 * n + 1) * NC + NC]
                src = getsrc(pt)[:, n * NC:(n + 1) * NC]
                dst = getdst(pt, n)
                tmp = C0["t0"][:, 0:NC]
                p.op("vector", None, lambda E, s=src, m=mu, t=tmp: E.tensor_sub(t, s, m))
                if affine is None:
                    p.op("vector", "sEV", lambda E, d=dst, r=rs, t=tmp: E.tensor_mul(d, t, r))
                else:
                    go, bo = affine
                    tmp2 = C0["t1"][:, 0:NC]
                    p.op("vector", None, lambda E, r=rs, t=tmp, t2=tmp2: E.tensor_mul(t2, t, r))
                    p.op("vector", "sEV", lambda E, d=dst, t2=tmp2, go=go, bo=bo, pt=pt:
                         E.tensor_scalar(out=d, in0=t2,
                                         scalar1=P["scol"][:, SC[go] + pt:SC[go] + pt + 1],
                                         scalar2=P["scol"][:, SC[bo] + pt:SC[bo] + pt + 1],
                                         op0=ALU.mult, op1=ALU.add))
            if chunk_cb:
                chunk_cb(n)

    def layer(R, w, nk, nm, ncols, rhs, evac, slabs=("sl0", "sl1"), banks=(0, 1),
              rhs_wait=None):
        if rhs_wait is not None:
            p.wait("tensor", rhs_wait[0], rhs_wait[1])
        for m in range(nm):
            buf = slabs[m % len(slabs)]
            uguard("sync", ("slab", id(R), buf))
            wv = p.dma("sync",
                       R[buf][:, 0:nk * 128],
                       stash[w][m * nk * 128 * 128:(m + 1) * nk * 128 * 128]
                       .rearrange("(p f) -> p f", p=128), "sW" + buf)
            p.wait("tensor", "sW" + buf, wv)
            bidx = banks[m % len(banks)]
            bguard("tensor", bidx)
            for k in range(nk):
                p.op("tensor", "sMM" if k == nk - 1 else None,
                     lambda E, b=bidx, bu=buf, k=k, f=(k == 0), l=(k == nk - 1):
                     E.matmul(ps[b][:, 0:ncols], R[bu][:, k * 128:(k + 1) * 128],
                              rhs(k), start=f, stop=l))
            evac(m, bidx, p.cnt["sMM"])
            buf_guard[("slab", id(R), buf)] = ("sMM", p.cnt["sMM"])

    def ffn_chunk(getx, getres, getout=None, out_dma=None):
        lnqb = lambda pt: R1["Gb"][:, pt * Q:(pt + 1) * Q]
        ln_stats(getx)
        ln_apply(getx, lambda pt, n: lnqb(pt)[:, n * NC:(n + 1) * NC])
        evq = p.cnt["sEV"]
        for m in range(MF):
            bufA = ("sl0", "sl1")[m % 2]
            bufB = ("sl2", "sl3")[m % 2]
            uguard("sync", ("slab", id(R1), bufA))
            uguard("sync", ("slab", id(R1), bufB))
            wva = p.dma("sync",
                        R1[bufA][:, 0:PD * 128],
                        stash["w1"][m * PD * 128 * 128:(m + 1) * PD * 128 * 128]
                        .rearrange("(p f) -> p f", p=128), "sW" + bufA)
            wvb = p.dma("sync",
                        R1[bufB][:, 0:PD * 128],
                        stash["w2"][m * PD * 128 * 128:(m + 1) * PD * 128 * 128]
                        .rearrange("(p f) -> p f", p=128), "sW" + bufB)
            p.wait("tensor", "sW" + bufA, wva)
            p.wait("tensor", "sW" + bufB, wvb)
            if m == 0:
                p.wait("tensor", "sEV", evq)
            bguard("tensor", 0)
            bguard("tensor", 1)
            for k in range(PD):
                p.op("tensor", None, lambda E, k=k, bu=bufA, f=(k == 0):
                     E.matmul(ps[0][:, 0:NC], R1[bu][:, k * 128:(k + 1) * 128],
                              lnqb(k)[:, 0:NC], start=f, stop=(k == PD - 1)))
            for k in range(PD):
                p.op("tensor", "sMM" if k == PD - 1 else None,
                     lambda E, k=k, bu=bufB, f=(k == 0):
                     E.matmul(ps[1][:, 0:NC], R1[bu][:, k * 128:(k + 1) * 128],
                              lnqb(k)[:, 0:NC], start=f, stop=(k == PD - 1)))
            mm = p.cnt["sMM"]
            sa = C0["t1"][:, (m % 2) * NC:(m % 2) * NC + NC]
            p.wait("scalar", "sMM", mm)
            uguard("scalar", ("t1", m % 2))
            c1 = p.op("scalar", "sSC", lambda E, sa=sa, m=m: E.activation(
                sa, ps[0][:, 0:NC], AF.Silu,
                scale=P["scol"][:, SC["s_w1"] + m:SC["s_w1"] + m + 1]))
            bank_guard[0] = ("sSC", c1)
            p.wait("vector", "sSC", c1)
            c2 = p.op("vector", "sEV", lambda E, sa=sa, m=m: E.scalar_tensor_tensor(
                out=R1["Hb"][:, m * Q:m * Q + NC], in0=ps[1][:, 0:NC],
                scalar=P["scol"][:, SC["s_w2"] + m:SC["s_w2"] + m + 1],
                in1=sa, op0=ALU.mult, op1=ALU.mult))
            bank_guard[1] = ("sEV", c2)
            buf_guard[("t1", m % 2)] = ("sEV", c2)
            buf_guard[("slab", id(R1), bufA)] = ("sMM", mm)
            buf_guard[("slab", id(R1), bufB)] = ("sMM", mm)

        def evac_gelu(dstbuf, scname):
            def ev(m, bidx, mm):
                p.wait("scalar", "sMM", mm)
                c = p.op("scalar", "sSC", lambda E, m=m, b=bidx: E.activation(
                    R1[dstbuf][:, m * Q:m * Q + NC], ps[b][:, 0:NC], AF.Gelu,
                    scale=P["scol"][:, SC[scname] + m:SC[scname] + m + 1]))
                bank_guard[bidx] = ("sSC", c)
            return ev
        hev = p.cnt["sEV"]
        layer(R1, "h1", MF, MF, NC, lambda k: R1["Hb"][:, k * Q:k * Q + NC],
              evac_gelu("Gb", "s_h1"), rhs_wait=("sEV", hev), banks=(0, 1, 2, 3))
        gsc = p.cnt["sSC"]
        layer(R1, "h2", MF, MF, NC, lambda k: R1["Gb"][:, k * Q:k * Q + NC],
              evac_gelu("Hb", "s_h2"), rhs_wait=("sSC", gsc), banks=(0, 1, 2, 3))

        def evac_w3(m, bidx, mm):
            dst = (getout or getres)(m)[:, 0:NC]
            res = getres(m)[:, 0:NC]
            p.wait("vector", "sMM", mm)
            c = p.op("vector", "sEV", lambda E, d=dst, r=res, m=m, b=bidx:
                     E.scalar_tensor_tensor(
                         out=d, in0=ps[b][:, 0:NC],
                         scalar=P["scol"][:, SC["hs3"] + m:SC["hs3"] + m + 1],
                         in1=r, op0=ALU.mult, op1=ALU.add))
            bank_guard[bidx] = ("sEV", c)
            if out_dma is not None:
                p.wait("gpsimd", "sEV", c)
                p.dma("gpsimd", out_dma(m), dst, "sOUT")
        layer(R1, "w3", MF, PD, NC, lambda k: R1["Hb"][:, k * Q:k * Q + NC], evac_w3,
              rhs_wait=("sSC", p.cnt["sSC"]), banks=(0, 1, 2, 3))

    def phase1():
        for q in range(T // Q):
            col0 = q * Q
            getx = lambda pt: R1["xq"][:, pt * Q:(pt + 1) * Q]
            if q > 0:
                p.wait("gpsimd", "sSP")
            lv = None
            for pt in range(PD):
                lv = p.dma("gpsimd", getx(pt),
                           h["xt"][pt * 128:(pt + 1) * 128, col0:col0 + Q], "sLD")
            for e in ("scalar", "vector", "tensor"):
                p.wait(e, "sLD", lv)
            ffn_chunk(getx, getx)
            ev = p.cnt["sEV"]
            p.wait("gpsimd", "sEV", ev)
            for pt in range(PD):
                p.dma("gpsimd", x1_sp[pt * 128:(pt + 1) * 128, col0:col0 + Q],
                      getx(pt), "sSP")
            xo = lambda pt: R1["Gf"][:, pt * Q:(pt + 1) * Q]
            ln_stats(getx)
            ln_apply(getx, lambda pt, n: xo(pt)[:, n * NC:(n + 1) * NC])
            ev = p.cnt["sEV"]
            p.wait("gpsimd", "sEV", ev)
            for pt in range(PD):
                p.dma("gpsimd", xo_sp[pt * 128:(pt + 1) * 128, col0:col0 + Q],
                      xo(pt), "sSP")
            ln_stats(xo)

            def spill_h(n):
                ev = p.cnt["sEV"]
                p.wait("gpsimd", "sEV", ev)
                for pt in range(PD):
                    p.dma("gpsimd", hT_sp[pt * 128:(pt + 1) * 128,
                                          col0 + n * NC:col0 + (n + 1) * NC],
                          R1["stgb"][:, pt * NC:(pt + 1) * NC], "sSP")
            ln_apply(xo, lambda pt, n: R1["stgb"][:, pt * NC:(pt + 1) * NC],
                     affine=("ng", "nb"), chunk_cb=spill_h)
            p.wait("vector", "sSP")
        p.wait("gpsimd", "sSP")
        p.barrier()

    def phase2_prep():
        g = "gpsimd"
        p.wait("sync", "sDM2", p.cnt["sDM2"])
        p.wait("gpsimd", "sDM2", p.cnt["sDM2"])
        p.dma(g, R2["eselS"][:], h["esel"][:], "sLD")
        for di, sfx in ((0, "1"), (1, "2")):
            p.dma(g, R2["xpT"][:, di * 16 * 96:(di + 1) * 16 * 96], h["xp" + sfx][:], "sLD")
            p.dma(g, R2["dtwT"][0:DTR, di * DI:(di + 1) * DI], h["dtw" + sfx][:], "sLD")
        mrow = R2["ubuf"][0:1, 0:NC]
        for wc in range(T // NC):
            mv = p.dma(g, mrow, h["mask"][:, wc * NC:(wc + 1) * NC], "sLD")
            p.wait("tensor", "sLD", mv)
            bguard("tensor", 6)
            mm = p.op("tensor", "sMM", lambda E: E.matmul(
                ps[6][:, 0:NC], P["onesr"][:], mrow, start=True, stop=True))
            p.wait("gpsimd", "sMM", mm)
            p.wait("scalar", "sMM", mm)
            c = p.op("scalar", "sSC", lambda E, wc=wc: E.copy(
                R2["maskbc"][:, wc * NC:(wc + 1) * NC], ps[6][:, 0:NC]))
            bank_guard[6] = ("sSC", c)
        p.barrier()
    def mamba_branch(dir_):
        w_in = ("in1", "in2")[dir_]
        w_out = ("out1", "out2")[dir_]
        xpT = R2["xpT"][:, dir_ * 16 * 96:(dir_ + 1) * 16 * 96]
        p.op("vector", "sEV", lambda E: E.memset(P["state"][:], 0.0))
        p.op("vector", "sEV", lambda E: E.memset(P["halo"][:], 0.0))
        pend = [None]

        def emit_out_proj(oc0, ygv):
            p.wait("tensor", "sEV", ygv)

            def evac_out(m, bidx, mm):
                p.wait("scalar", "sMM", mm)
                uguard("scalar", ("ost", m % 2))
                cc = p.op("scalar", "sSC", lambda E, m=m, b=bidx: E.copy(
                    R2["ost"][:, (m % 2) * CH:(m % 2) * CH + CH], ps[b][:, 0:CH]))
                bank_guard[bidx] = ("sSC", cc)
                p.wait("gpsimd", "sSC", cc)
                dv = p.dma("gpsimd", mout[dir_][m * 128:(m + 1) * 128, oc0:oc0 + CH],
                           R2["ost"][:, (m % 2) * CH:(m % 2) * CH + CH],
                           "sOS%d" % (m % 2))
                buf_guard[("ost", m % 2)] = ("sOS%d" % (m % 2), dv)
            layer(R2, w_out, NDB, PD, CH,
                  lambda k: R2["ygb"][:, k * CH:(k + 1) * CH],
                  evac_out, banks=(6, 7))
            buf_guard["ygb"] = ("sMM", p.cnt["sMM"])

        for c in range(NCH):
            pos0 = c * CH
            ext = c >= EXT0
            srcc = (pos0, pos0 + CH) if dir_ == 0 else (T - pos0 - CH, T - pos0)
            uguard("gpsimd", "hwin")
            lv = None
            for pt in range(PD):
                lv = p.dma("gpsimd", R2["hwin"][:, pt * CH:(pt + 1) * CH],
                           hT_sp[pt * 128:(pt + 1) * 128, srcc[0]:srcc[1]], "sHW")
            rhsbuf = R2["hwin"]
            if dir_ == 1:
                p.wait("vector", "sHW", lv)
                uguard("vector", "hrev")
                for pt in range(PD):
                    p.op("vector", "sEV" if pt == PD - 1 else None,
                         lambda E, pt=pt: E.tensor_copy(
                             R2["hrev"][:, pt * CH:(pt + 1) * CH],
                             R2["hwin"][:, pt * CH:(pt + 1) * CH][:, ::-1]))
                buf_guard["hwin"] = ("sEV", p.cnt["sEV"])
                rhsbuf = R2["hrev"]
                p.wait("tensor", "sEV")
            else:
                p.wait("tensor", "sHW", lv)

            def evac_in(m, bidx, mm):
                p.wait("scalar", "sMM", mm)
                if m < NDB:
                    cc = p.op("scalar", "sSC", lambda E, m=m, b=bidx: E.copy(
                        R2["xspre"][:, m * (CH + 4) + 3:m * (CH + 4) + 3 + CH],
                        ps[b][:, 0:CH]))
                    bank_guard[bidx] = ("sSC", cc)
                else:
                    uguard("scalar", ("zst", m % 2))
                    cc = p.op("scalar", "sSC", lambda E, m=m, b=bidx: E.copy(
                        R2["zst"][:, (m % 2) * CH:(m % 2) * CH + CH], ps[b][:, 0:CH]))
                    bank_guard[bidx] = ("sSC", cc)
                    p.wait("gpsimd", "sSC", cc)
                    dv = p.dma("gpsimd",
                               z_sp[dir_ * DI + (m - NDB) * 128:
                                    dir_ * DI + (m - NDB) * 128 + 128, pos0:pos0 + CH],
                               R2["zst"][:, (m % 2) * CH:(m % 2) * CH + CH],
                               "sZS%d" % (m % 2))
                    buf_guard[("zst", m % 2)] = ("sZS%d" % (m % 2), dv)
            p.wait("scalar", "sEV", p.cnt["sEV"])
            layer(R2, w_in, PD, 2 * NDB, CH,
                  lambda k: rhsbuf[:, k * CH:(k + 1) * CH], evac_in,
                  banks=(0, 1, 2, 3))
            inp_mm = p.cnt["sMM"]

            if pend[0] is not None:
                emit_out_proj(*pend[0])
                pend[0] = None

            evsc = p.cnt["sSC"]
            p.wait("vector", "sSC", evsc)
            for db in range(NDB):
                p.op("vector", None, lambda E, db=db: E.tensor_copy(
                    R2["xspre"][:, db * (CH + 4):db * (CH + 4) + 3],
                    P["halo"][:, db * 4:db * 4 + 3]))
                p.op("vector", None, lambda E, db=db: E.tensor_mul(
                    R2["xspre"][:, db * (CH + 4) + 3:db * (CH + 4) + 3 + CH],
                    R2["xspre"][:, db * (CH + 4) + 3:db * (CH + 4) + 3 + CH],
                    R2["maskbc"][:, pos0:pos0 + CH]))
            for db in range(NDB):
                seg = R2["xspre"][:, db * (CH + 4):db * (CH + 4) + CH + 4]
                acc = R2["xsconv"][:, db * CH:(db + 1) * CH]
                p.op("vector", None, lambda E, s=seg, a=acc, db=db: E.tensor_scalar(
                    out=a, in0=s[:, 0:CH],
                    scalar1=P["cwcol"][:, dir_ * NDB * 4 + db * 4:dir_ * NDB * 4 + db * 4 + 1],
                    scalar2=None, op0=ALU.mult))
                for k in range(1, 4):
                    p.op("vector", "sEV" if k == 3 else None,
                         lambda E, s=seg, a=acc, db=db, k=k: E.scalar_tensor_tensor(
                             out=a, in0=s[:, k:k + CH],
                             scalar=P["cwcol"][:, dir_ * NDB * 4 + db * 4 + k:
                                               dir_ * NDB * 4 + db * 4 + k + 1],
                             in1=a, op0=ALU.mult, op1=ALU.add))
                ev = p.cnt["sEV"]
                p.wait("scalar", "sEV", ev)
                p.op("scalar", "sSC", lambda E, a=acc, db=db: E.activation(
                    a, a, AF.Silu,
                    bias=P["cbcol"][:, dir_ * NDB + db:dir_ * NDB + db + 1]))
            convsc = p.cnt["sSC"]
            for db in range(NDB):
                p.op("vector", None, lambda E, db=db: E.tensor_copy(
                    P["halo"][:, db * 4:db * 4 + 3],
                    R2["xspre"][:, db * (CH + 4) + CH:db * (CH + 4) + CH + 3]))
            p.op("vector", "sEV", lambda E: E.engine_nop())

            p.wait("vector", "sSC", convsc)
            uguard("vector", "xsbf")
            for db in range(NDB):
                p.op("vector", "sEV" if db == NDB - 1 else None,
                     lambda E, db=db: E.tensor_copy(
                         R2["xsbf"][:, db * CH:(db + 1) * CH],
                         R2["xsconv"][:, db * CH:(db + 1) * CH]))
            xbv = p.cnt["sEV"]
            p.wait("tensor", "sEV", xbv)
            bguard("tensor", 4)
            for k in range(NDB):
                p.op("tensor", "sMM" if k == NDB - 1 else None,
                     lambda E, k=k, f=(k == 0), l=(k == NDB - 1): E.matmul(
                         ps[4][0:80, 0:CH], xpT[:, k * 96:k * 96 + 80],
                         R2["xsbf"][:, k * CH:(k + 1) * CH], start=f, stop=l))
            if ext:
                bguard("tensor", 5)
                for k in range(NDB):
                    p.op("tensor", "sMM" if k == NDB - 1 else None,
                         lambda E, k=k, f=(k == 0), l=(k == NDB - 1): E.matmul(
                             ps[5][0:16, 0:CH], xpT[:, k * 96 + 80:k * 96 + 96],
                             R2["xsbf"][:, k * CH:(k + 1) * CH], start=f, stop=l))
            mm2 = p.cnt["sMM"]
            p.wait("scalar", "sMM", mm2)
            p.op("scalar", "sSC", lambda E: E.copy(
                R2["dbl"][0:80, 0:CH], ps[4][0:80, 0:CH]))
            c1 = p.op("scalar", "sSC", lambda E: E.copy(
                R2["brow"][0:DS, 0:CH], ps[4][64:80, 0:CH]))
            bank_guard[4] = ("sSC", c1)
            if ext:
                c2 = p.op("scalar", "sSC", lambda E: E.copy(
                    R2["crow"][0:16, 0:CH], ps[5][0:16, 0:CH]))
                bank_guard[5] = ("sSC", c2)
            sc_dbl = p.cnt["sSC"]
            p.wait("vector", "sSC", sc_dbl)
            dtc = p.op("vector", "sEV", lambda E: E.tensor_copy(
                R2["dblbf"][0:DTR, 0:CH], R2["dbl"][0:DTR, 0:CH]))
            p.wait("tensor", "sEV", dtc)

            for db in range(NDB):
                bguard("tensor", 4)
                mm = p.op("tensor", "sMM", lambda E, db=db: E.matmul(
                    ps[4][:, 0:CH],
                    R2["dtwT"][0:DTR, dir_ * DI + db * 128:dir_ * DI + (db + 1) * 128],
                    R2["dblbf"][0:DTR, 0:CH], start=True, stop=True))
                p.wait("vector", "sMM", mm)
                cc = p.op("vector", "sEV", lambda E, db=db: E.tensor_scalar(
                    out=R2["dtb"][:, db * CH:(db + 1) * CH], in0=ps[4][:, 0:CH],
                    scalar1=P["dtbcol"][:, dir_ * NDB + db:dir_ * NDB + db + 1],
                    scalar2=85.0, op0=ALU.add, op1=ALU.min))
                bank_guard[4] = ("sEV", cc)
            dtev0 = p.cnt["sEV"]
            p.wait("scalar", "sEV", dtev0)
            p.op("scalar", None, lambda E: E.activation(
                R2["ubuf"][:], R2["dtb"][:], AF.Sigmoid, scale=P["negcol"][:, 0:1]))
            lnc = p.op("scalar", "sSC", lambda E: E.activation(
                R2["ubuf"][:], R2["ubuf"][:], AF.Ln))
            p.wait("vector", "sSC", lnc)
            mb3 = R2["maskbc"][:, pos0:pos0 + CH][:, None, :].broadcast_to([128, NDB, CH])
            p.op("vector", None, lambda E: E.scalar_tensor_tensor(
                out=R2["dtb"].rearrange("p (d t) -> p d t", d=NDB),
                in0=R2["ubuf"].rearrange("p (d t) -> p d t", d=NDB),
                scalar=P["negcol"][:, 0:1], in1=mb3, op0=ALU.mult, op1=ALU.mult))
            p.op("vector", "sEV", lambda E: E.tensor_mul(
                R2["ubuf"][:], R2["dtb"][:], R2["xsconv"][:]))
            dt_ev = p.cnt["sEV"]

            PL = SGRP * CH

            def dap(par):
                return R2["daP"][:, par * PL:(par + 1) * PL]

            def dbxp(par):
                return R2["dbxP"][:, par * PL:(par + 1) * PL]

            def hp(par):
                return R2["hP"][:, par * PL:(par + 1) * PL]
            bbc3 = R2["bbcP"].rearrange("p (s t) -> p s t", s=SGRP)
            for g in range(NG):
                p.wait("tensor", "sSC", sc_dbl)
                uguard("tensor", ("bbcP",))
                for si in range(SGRP):
                    s = g * SGRP + si
                    bguard("tensor", 5)
                    mm = p.op("tensor", "sMM", lambda E, s=s: E.matmul(
                        ps[5][:, 0:CH], R2["eselS"][:, s * 128:(s + 1) * 128],
                        R2["brow"][0:DS, 0:CH], start=True, stop=True))
                    p.wait("scalar", "sMM", mm)
                    cb = p.op("scalar", "sSC", lambda E, si=si: E.copy(
                        R2["bbcP"][:, si * CH:(si + 1) * CH], ps[5][:, 0:CH]))
                    bank_guard[5] = ("sSC", cb)
                    if ext:
                        bguard("tensor", 6)
                        mm = p.op("tensor", "sMM", lambda E, s=s: E.matmul(
                            ps[6][:, 0:CH], R2["eselS"][:, s * 128:(s + 1) * 128],
                            R2["crow"][0:16, 0:CH], start=True, stop=True))
                        p.wait("scalar", "sMM", mm)
                        ccp = p.op("scalar", "sSC", lambda E, si=si: E.copy(
                            R2["cbcP"][:, si * CH:(si + 1) * CH], ps[6][:, 0:CH]))
                        bank_guard[6] = ("sSC", ccp)
                bc_sc = p.cnt["sSC"]
                for db in range(NDB):
                    par = db % 2
                    da = dap(par)
                    dbx = dbxp(par)
                    hh = hp(par)
                    pp = R2["pP"][:, par * PL:(par + 1) * PL]
                    p.wait("scalar", "sEV", dt_ev)
                    uguard("scalar", ("daP", par))
                    for si in range(SGRP):
                        s = g * SGRP + si
                        p.op("scalar", "sSC" if si == SGRP - 1 else None,
                             lambda E, db=db, s=s, si=si, da=da: E.activation(
                                 da[:, si * CH:(si + 1) * CH],
                                 R2["dtb"][:, db * CH:(db + 1) * CH], AF.Exp,
                                 scale=P["acol"][:, dir_ * NDB * DS + db * DS + s:
                                                 dir_ * NDB * DS + db * DS + s + 1]))
                    da_sc = p.cnt["sSC"]
                    p.wait("vector", "sSC", bc_sc)
                    uguard("vector", ("dbxP", par))
                    p.op("vector", None, lambda E, db=db, dbx=dbx: E.tensor_mul(
                        dbx.rearrange("p (s t) -> p s t", s=SGRP),
                        R2["ubuf"][:, db * CH:(db + 1) * CH][:, None, :]
                        .broadcast_to([128, SGRP, CH]),
                        bbc3))
                    p.wait("vector", "sSC", da_sc)
                    uguard("vector", ("hP", par))
                    for si in range(SGRP):
                        s = g * SGRP + si
                        p.op("vector", "sEV" if si == SGRP - 1 else None,
                             lambda E, db=db, s=s, si=si, da=da, dbx=dbx, hh=hh:
                             E.tensor_tensor_scan(
                                 hh[:, si * CH:(si + 1) * CH],
                                 da[:, si * CH:(si + 1) * CH],
                                 dbx[:, si * CH:(si + 1) * CH],
                                 P["state"][:, db * DS + s:db * DS + s + 1],
                                 ALU.mult, ALU.add))
                    scan_ev = p.cnt["sEV"]
                    buf_guard[("daP", par)] = ("sEV", scan_ev + 1)
                    stv = p.op("vector", "sEV", lambda E, db=db, g=g, hh=hh: E.tensor_copy(
                        P["state"][:, db * DS + g * SGRP:db * DS + (g + 1) * SGRP],
                        hh[:, CH - 1:SGRP * CH:CH]))
                    if ext:
                        p.wait("gpsimd", "sEV", stv)
                        p.wait("gpsimd", "sSC", bc_sc)
                        uguard("gpsimd", ("pP", par))
                        p.op("gpsimd", None, lambda E, hh=hh, pp=pp: E.tensor_mul(
                            pp, hh, R2["cbcP"]))
                        p.op("gpsimd", None, lambda E, pp=pp: E.tensor_add(
                            pp[:, 0:CH], pp[:, 0:CH], pp[:, CH:2 * CH]))
                        p.op("gpsimd", None, lambda E, pp=pp: E.tensor_add(
                            pp[:, 2 * CH:3 * CH], pp[:, 2 * CH:3 * CH],
                            pp[:, 3 * CH:4 * CH]))
                        if g == 0:
                            rv = p.op("gpsimd", "sGP", lambda E, db=db, pp=pp: E.tensor_add(
                                R2["yb"][:, db * CH:(db + 1) * CH],
                                pp[:, 0:CH], pp[:, 2 * CH:3 * CH]))
                        else:
                            p.op("gpsimd", None, lambda E, pp=pp: E.tensor_add(
                                pp[:, 0:CH], pp[:, 0:CH], pp[:, 2 * CH:3 * CH]))
                            rv = p.op("gpsimd", "sGP", lambda E, db=db, pp=pp: E.tensor_add(
                                R2["yb"][:, db * CH:(db + 1) * CH],
                                R2["yb"][:, db * CH:(db + 1) * CH], pp[:, 0:CH]))
                        buf_guard[("hP", par)] = ("sGP", rv)
                        buf_guard[("pP", par)] = ("sGP", rv)
                        buf_guard[("dbxP", par)] = ("sGP", rv)
                    else:
                        buf_guard[("hP", par)] = ("sEV", stv)
                        buf_guard[("dbxP", par)] = ("sEV", stv)
                buf_guard[("bbcP",)] = ("sEV", p.cnt["sEV"])

            if ext:
                p.wait("vector", "sGP", p.cnt["sGP"])
                p.wait("gpsimd", "sZS0", p.cnt["sZS0"])
                p.wait("gpsimd", "sZS1", p.cnt["sZS1"])
                p.wait("gpsimd", "sEV", p.cnt["sEV"])
                for db in range(NDB):
                    uguard("gpsimd", ("zrd", db % 2))
                    zv = p.dma("gpsimd", R2["zrd"][:, (db % 2) * CH:(db % 2) * CH + CH],
                               z_sp[dir_ * DI + db * 128:dir_ * DI + (db + 1) * 128,
                                    pos0:pos0 + CH], "sZR%d" % (db % 2))
                    p.op("vector", None, lambda E, db=db: E.scalar_tensor_tensor(
                        out=R2["yb"][:, db * CH:(db + 1) * CH],
                        in0=R2["xsconv"][:, db * CH:(db + 1) * CH],
                        scalar=P["dpcol"][:, dir_ * NDB + db:dir_ * NDB + db + 1],
                        in1=R2["yb"][:, db * CH:(db + 1) * CH],
                        op0=ALU.mult, op1=ALU.add))
                    p.wait("scalar", "sZR%d" % (db % 2), zv)
                    uguard("scalar", ("szb",))
                    szc = p.op("scalar", "sSC", lambda E, db=db: E.activation(
                        R2["szb"][:, 0:CH], R2["zrd"][:, (db % 2) * CH:(db % 2) * CH + CH],
                        AF.Silu))
                    buf_guard[("zrd", db % 2)] = ("sSC", szc)
                    buf_guard[("daP", 0)] = ("sSC", szc)
                    p.wait("vector", "sSC", szc)
                    uguard("vector", "ygb")
                    yv = p.op("vector", "sEV", lambda E, db=db: E.tensor_mul(
                        R2["ygb"][:, db * CH:(db + 1) * CH],
                        R2["yb"][:, db * CH:(db + 1) * CH], R2["szb"][:, 0:CH]))
                    buf_guard[("szb",)] = ("sEV", yv)
                pend[0] = ((c - EXT0) * CH, p.cnt["sEV"])
            buf_guard["hwin"] = ("sMM", inp_mm)
            buf_guard["hrev"] = ("sMM", inp_mm)
            buf_guard["xsbf"] = ("sMM", mm2)
        if pend[0] is not None:
            emit_out_proj(*pend[0])
            pend[0] = None


    def phase3():
        for q in range(OWN // Q):
            bc0 = T // 2 + q * Q
            oc0 = q * Q
            Afn = lambda pt: R1["Hf"][:, pt * Q:(pt + 1) * Q]
            Bfn = lambda pt: R1["Hf"][:, (PD + pt) * Q:(PD + pt + 1) * Q]
            xofn = lambda pt: R1["Gf"][:, pt * Q:(pt + 1) * Q]
            x2fn = lambda pt: R1["Gf"][:, (PD + pt) * Q:(PD + pt + 1) * Q]
            x1fn = lambda pt: R1["xq"][:, pt * Q:(pt + 1) * Q]
            p.wait("gpsimd", "sEV")
            p.wait("gpsimd", "sOUT")
            lv = None
            for pt in range(PD):
                p.dma("gpsimd", Afn(pt), mout[0][pt * 128:(pt + 1) * 128, oc0:oc0 + Q], "sLD")
                p.dma("gpsimd", Bfn(pt), mout[1][pt * 128:(pt + 1) * 128, oc0:oc0 + Q], "sLD")
                p.dma("gpsimd", xofn(pt), xo_sp[pt * 128:(pt + 1) * 128, bc0:bc0 + Q], "sLD")
                lv = p.dma("gpsimd", x1fn(pt), x1_sp[pt * 128:(pt + 1) * 128, bc0:bc0 + Q], "sLD")
            p.wait("vector", "sLD", lv)
            for pt in range(PD):
                p.op("vector", None, lambda E, pt=pt: E.tensor_add(
                    Afn(pt)[:], Afn(pt)[:], Bfn(pt)[:]))
                p.op("vector", "sEV", lambda E, pt=pt: E.tensor_add(
                    Afn(pt)[:], Afn(pt)[:], xofn(pt)[:]))
            ev = p.cnt["sEV"]
            p.wait("scalar", "sEV", ev)
            for pt in range(PD):
                p.op("scalar", "sSC", lambda E, pt=pt: E.activation(
                    Afn(pt)[:], Afn(pt)[:], AF.Gelu))
            sc = p.cnt["sSC"]
            p.wait("vector", "sSC", sc)
            spv = None
            for pt in range(PD):
                cv = p.op("vector", "sEV", lambda E, pt=pt: E.tensor_add(
                    x2fn(pt)[:], xofn(pt)[:], Afn(pt)[:]))
                p.wait("gpsimd", "sEV", cv)
                spv = p.dma("gpsimd", x2_sp[pt * 128:(pt + 1) * 128, oc0:oc0 + Q],
                            x2fn(pt), "sSP")
            ln_stats(x2fn)
            hfb = lambda pt: R1["Hb"][:, pt * Q:(pt + 1) * Q]
            ln_apply(x2fn, lambda pt, n: hfb(pt)[:, n * NC:(n + 1) * NC],
                     affine=("fg", "fb"))
            ev = p.cnt["sEV"]
            p.wait("tensor", "sEV", ev)

            def evac_ff1(m, bidx, mm):
                p.wait("scalar", "sMM", mm)
                cc = p.op("scalar", "sSC", lambda E, m=m, b=bidx: E.activation(
                    R1["Gb"][:, m * Q:m * Q + NC], ps[b][:, 0:NC], AF.Gelu,
                    bias=P["scol"][:, SC["ff1b"] + m:SC["ff1b"] + m + 1]))
                bank_guard[bidx] = ("sSC", cc)
            p.wait("scalar", "sSP", spv)
            layer(R1, "ff1", PD, MF, NC, lambda k: hfb(k)[:, 0:NC], evac_ff1,
                  banks=(0, 1, 2, 3))

            p.wait("gpsimd", "sSP", spv)

            def evac_ff2(m, bidx, mm):
                uguard("gpsimd", ("stg", m % 2))
                lx = p.dma("gpsimd", R1["stg"][:, (m % 2) * NC:(m % 2) * NC + NC],
                           x2_sp[m * 128:(m + 1) * 128, oc0:oc0 + NC], "sST%d" % (m % 2))
                p.wait("vector", "sST%d" % (m % 2), lx)
                p.wait("vector", "sMM", mm)
                p.op("vector", None, lambda E, m=m, b=bidx: E.scalar_tensor_tensor(
                    out=C0["t0"][:, 0:NC], in0=ps[b][:, 0:NC],
                    scalar=P["scol"][:, SC["ff2b"] + m:SC["ff2b"] + m + 1],
                    in1=R1["stg"][:, (m % 2) * NC:(m % 2) * NC + NC],
                    op0=ALU.add, op1=ALU.add))
                cc = p.op("vector", "sEV", lambda E, m=m: E.tensor_add(
                    x1fn(m)[:, 0:NC], x1fn(m)[:, 0:NC], C0["t0"][:, 0:NC]))
                bank_guard[bidx] = ("sEV", cc)
                buf_guard[("stg", m % 2)] = ("sEV", cc)
            layer(R1, "ff2", MF, PD, NC, lambda k: R1["Gb"][:, k * Q:k * Q + NC],
                  evac_ff2, rhs_wait=("sSC", p.cnt["sSC"]), banks=(0, 1, 2, 3))

            ffn_chunk(x1fn, x1fn,
                      out_dma=lambda m: out_h[m * 128:(m + 1) * 128, oc0:oc0 + Q])
        p.wait("gpsimd", "sOUT")
        p.barrier()

    phase0()
    if _PHASES >= 1:
        phase1()
    if _PHASES >= 2:
        phase2_prep()
        mamba_branch(0)
        p.barrier()
        mamba_branch(1)
        p.barrier()
    if _PHASES >= 3:
        phase3()

    with nc.Block() as block:
        for name, dec in (("sync", block.sync), ("tensor", block.tensor),
                          ("vector", block.vector), ("scalar", block.scalar),
                          ("gpsimd", block.gpsimd)):
            def make(name):
                def emitall(E):
                    for fn in p.q[name]:
                        fn(E)
                return emitall
            dec(make(name))
    return nc



def _tile_w(W):
    O, I = W.shape
    Wt = np.ascontiguousarray(np.asarray(W, np.float32).T)
    a = Wt.reshape(I // 128, 128, O // 128, 128).transpose(2, 1, 0, 3)
    return np.ascontiguousarray(a).reshape(-1)


def _cols(v):
    v = np.asarray(v, np.float32).reshape(-1)
    n = v.size // 128
    return np.ascontiguousarray(v.reshape(n, 128).T)


def _cols2(M):
    M = np.asarray(M, np.float32)
    dI, k = M.shape
    return np.ascontiguousarray(M.reshape(NDB, 128, k).transpose(1, 0, 2).reshape(128, NDB * k))


_WCACHE = {}


def make_weight_feed(d):
    key = id(d)
    if key in _WCACHE:
        return _WCACHE[key]
    feed = {}
    es = np.zeros((DS, DS * 128), np.float32)
    for s in range(DS):
        es[s, s * 128:(s + 1) * 128] = 1.0
    feed["esel"] = es
    wmap = {
        "w1": d["L0_ffn_w1_w"], "w2": d["L0_ffn_w2_w"], "h1": d["L0_ffn_h1_w"],
        "h2": d["L0_ffn_h2_w"], "w3": d["L0_ffn_w3_w"], "ff1": d["L0_attn_ff1_w"],
        "ff2": d["L0_attn_ff2_w"], "in1": d["L0_attn_m1_in_w"],
        "in2": d["L0_attn_m2_in_w"], "out1": d["L0_attn_m1_out_w"],
        "out2": d["L0_attn_m2_out_w"],
    }
    for k, W in wmap.items():
        feed[f"w_{k}"] = _tile_w(W)
    for di, sfx in ((1, "1"), (2, "2")):
        xp = np.asarray(d[f"L0_attn_m{di}_xproj_w"], np.float32)
        xpT = np.ascontiguousarray(xp.T)
        feed[f"xp{sfx}"] = np.ascontiguousarray(
            xpT.reshape(16, 128, 96).transpose(1, 0, 2).reshape(128, 16 * 96))
        feed[f"dtw{sfx}"] = np.ascontiguousarray(
            np.asarray(d[f"L0_attn_m{di}_dt_w"], np.float32).T)
        feed[f"alog{sfx}"] = _cols2(d[f"L0_attn_m{di}_Alog"])
        feed[f"cw{sfx}"] = _cols2(d[f"L0_attn_m{di}_conv_w"])
        feed[f"cb{sfx}"] = _cols(d[f"L0_attn_m{di}_conv_b"])
        feed[f"dtb{sfx}"] = _cols(d[f"L0_attn_m{di}_dt_b"])
        feed[f"dp{sfx}"] = _cols(d[f"L0_attn_m{di}_D"])
    feed["s_w1"] = _cols(d["L0_ffn_w1_s"])
    feed["s_w2"] = _cols(d["L0_ffn_w2_s"])
    feed["s_h1"] = _cols(d["L0_ffn_h1_s"])
    feed["s_h2"] = _cols(d["L0_ffn_h2_s"])
    feed["s_w3"] = _cols(d["L0_ffn_w3_s"])
    feed["ng"] = _cols(d["L0_attn_norm_g"])
    feed["nb"] = _cols(d["L0_attn_norm_b"])
    feed["fg"] = _cols(d["L0_attn_ffn_g"])
    feed["fb"] = _cols(d["L0_attn_ffn_b"])
    feed["ff1b"] = _cols(d["L0_attn_ff1_b"])
    feed["ff2b"] = _cols(d["L0_attn_ff2_b"])
    _WCACHE[key] = feed
    return feed


def make_inputs(d, b, j, T=4096):
    x = np.asarray(d["L0_attn_x"][b], np.float32)
    if j == 0:
        x = np.concatenate([x[T // 2:], x[:T // 2]], 0)
    feed = {"xt": np.ascontiguousarray(x.T)}
    m = np.ones((1, T), np.float32)
    if j == 0:
        m[0, :T // 2] = 0.0
    feed["mask"] = m
    feed.update(make_weight_feed(d))
    return feed


def kernel(**inputs):
    T = 4096
    nc = build(T)
    in_maps = [make_inputs(inputs, c // 2, c % 2, T) for c in range(8)]
    res = run_bass_kernel_spmd(nc, in_maps, core_ids=list(range(8)))
    B, L = inputs["L0_attn_x"].shape[0], inputs["L0_attn_x"].shape[1]
    out = np.zeros((B, L, D), np.float32)
    for c in range(8):
        b, j = c // 2, c % 2
        o = np.asarray(res.results[c]["out"])
        out[b, j * (L // 2):(j + 1) * (L // 2)] = o.T
    return out
